# version 40
# baseline (speedup 1.0000x reference)
import numpy as np
import ml_dtypes
import concourse.bass as bass
import concourse.mybir as mybir
from concourse.bass_core_rust import IndirectOffsetOnAxis
from concourse.bass_utils import run_bass_kernel_spmd

F32 = mybir.dt.float32
I32 = mybir.dt.int32
BF16 = mybir.dt.bfloat16
AF = mybir.ActivationFunctionType
ALU = mybir.AluOpType
AX = mybir.AxisListType
NEG = -1.0e30


def full_cfg():
    return dict(n_cores=8, n_tok=2048, d=1024, H=16, dh=64, FF=2048, E=8,
                search_iters=32)


class Prog:

    DMA_RING = 4

    def __init__(self):
        self.streams = {n: [] for n in ("t", "v", "s", "q", "g")}
        self.cnt = {}
        for n in ("t", "v", "s", "cc"):
            self.cnt[n] = 0
        for qn in ("q", "g"):
            for r in range(self.DMA_RING):
                self.cnt[f"{qn}{r}"] = 0
        self.dma_i = {"q": 0, "g": 0}
        self.state = {}

    def op(self, st, emit, reads=(), writes=(), kind="compute"):
        waits = {}

        def need(sn, val):
            if val > waits.get(sn, 0):
                waits[sn] = val

        if kind == "dma":
            i = self.dma_i[st]
            self.dma_i[st] = i + 1
            sem, inc = f"{st}{i % self.DMA_RING}", 16
            if i >= self.DMA_RING:
                need(sem, self.cnt[sem])
        elif kind == "cc":
            sem, inc = "cc", 1
        else:
            sem, inc = st, 1

        for k in reads:
            w = self.state.get(k)
            if w and w[0] is not None:
                need(*w[0])
        for k in writes:
            w = self.state.get(k)
            if w:
                if w[0] is not None:
                    need(*w[0])
                for sn, val in w[1].items():
                    need(sn, val)
        if st == "t" and kind == "compute":
            waits.pop("t", None)
        self.cnt[sem] += inc
        tag = (sem, self.cnt[sem])
        self.streams[st].append((waits, emit, sem, inc))
        for k in writes:
            self.state[k] = [tag, {}]
        for k in reads:
            w = self.state.setdefault(k, [None, {}])
            w[1][tag[0]] = tag[1]
        return tag


def _replay(ops, eng, sems):
    emitted = {}
    for waits, emit, sem, inc in ops:
        for sn in sorted(waits):
            val = waits[sn]
            if emitted.get(sn, 0) < val:
                eng.wait_ge(sems[sn], val)
                emitted[sn] = val
        emit(eng).then_inc(sems[sem], inc)


def build_program(cfg):
    n_tok = cfg["n_tok"]; d = cfg["d"]; H = cfg["H"]; dh = cfg["dh"]
    FF = cfg["FF"]; E = cfg["E"]; NC = cfg["n_cores"]
    TT = n_tok // 128
    KC = d // 128
    KT = FF // 128
    KI = n_tok // 128
    QBW = min(512, n_tok)
    QB = n_tok // QBW
    T_glob = NC * n_tok
    cap = T_glob // E
    PPE = 128 // E
    HFC = PPE // NC
    U = n_tok // HFC
    PRW = (T_glob * E) // 128
    HPG = min(4, H)
    NG = H // HPG
    DBLK = max(1, d // 512)
    DW = min(512, d)
    TC = min(512, n_tok)
    NTC = n_tok // TC
    SW = HPG * dh
    PAD = max(128, (3 * n_tok) // 16)
    NP4 = PAD // 128

    nc = bass.Bass("TRN2", target_bir_lowering=False)

    ap = lambda t: t.ap()
    x_in = ap(nc.dram_tensor("x", [n_tok, d], F32, kind="ExternalInput"))
    wkqv = nc.dram_tensor("wkqv", [d, 3 * d], F32, kind="ExternalInput")
    wproj = nc.dram_tensor("wproj", [d, d], F32, kind="ExternalInput")
    wg = nc.dram_tensor("wg", [d, E], F32, kind="ExternalInput")
    w1 = nc.dram_tensor("w1", [E, d, FF], F32, kind="ExternalInput")
    w2 = nc.dram_tensor("w2", [E, FF, d], F32, kind="ExternalInput")
    c_idb = ap(nc.dram_tensor("c_idb", [128, 128], BF16, kind="ExternalInput"))
    c_idf = ap(nc.dram_tensor("c_idf", [128, 128], F32, kind="ExternalInput"))
    c_cm = ap(nc.dram_tensor("c_cm", [128, 128], F32, kind="ExternalInput"))
    c_eio = ap(nc.dram_tensor("c_eio", [128, E], F32, kind="ExternalInput"))
    c_eio99 = ap(nc.dram_tensor("c_eio99", [128, E], F32, kind="ExternalInput"))
    c_kidx = ap(nc.dram_tensor("c_kidx", [128, KI], F32, kind="ExternalInput"))
    c_iota = ap(nc.dram_tensor("c_iota", [1, n_tok], F32, kind="ExternalInput"))
    c_ones1f = ap(nc.dram_tensor("c_ones1f", [1, 128], F32, kind="ExternalInput"))
    c_eps = ap(nc.dram_tensor("c_eps", [128, 1], F32, kind="ExternalInput"))
    c_slb = ap(nc.dram_tensor("c_slb", [128, H], F32, kind="ExternalInput"))
    c_bkq = ap(nc.dram_tensor("c_bkq", [128, H], F32, kind="ExternalInput"))
    c_bv = ap(nc.dram_tensor("c_bv", [1, d], F32, kind="ExternalInput"))
    c_bp = ap(nc.dram_tensor("c_bp", [1, d], F32, kind="ExternalInput"))
    c_ones2b = ap(nc.dram_tensor("c_ones2b", [1, n_tok], BF16, kind="ExternalInput"))
    c_trils = ap(nc.dram_tensor("c_trils", [128, 128], BF16, kind="ExternalInput"))
    c_zeros = nc.dram_tensor("c_zeros", [128, d], BF16, kind="ExternalInput")
    y_out = ap(nc.dram_tensor("y", [n_tok, d], F32, kind="ExternalOutput"))

    p123d = nc.dram_tensor("p123d", [3, H * n_tok], BF16, kind="Internal")
    cc_in = nc.dram_tensor("cc_in", [E, n_tok], F32, kind="Internal")
    cc_out = nc.dram_tensor("cc_out", [NC * E, n_tok], F32, kind="Internal",
                            addr_space="Shared")
    thd = nc.dram_tensor("thd", [1, 128], F32, kind="Internal")
    prd = nc.dram_tensor("prd", [128, (NC * n_tok * E) // 128], F32, kind="Internal")
    xnd = nc.dram_tensor("xnd", [n_tok, d], BF16, kind="Internal")
    xgd = nc.dram_tensor("xgd", [E * PAD, d], BF16, kind="Internal")
    xh2d = nc.dram_tensor("xh2d", [E * PAD, d], BF16, kind="Internal")

    P = Prog()

    def collect(fn):
        ops = []
        real_op = P.op
        P.op = lambda *a, **k: ops.append((a, k))
        try:
            fn()
        finally:
            del P.op
        return [(lambda a=a, k=k: real_op(*a, **k)) for (a, k) in ops]
    from contextlib import ExitStack
    stack = ExitStack()
    sb = lambda name, shape, dt: stack.enter_context(nc.sbuf_tensor(name, shape, dt))
    pt = lambda name, shape: stack.enter_context(nc.psum_tensor(name, shape, F32))

    with stack:
        ACC = [sb(f"acc{t}", [128, d], F32) for t in range(TT)]
        STG = [sb("stg0", [128, 1024], F32)]
        IDB = sb("idb", [128, 128], BF16)
        IDF = sb("idf", [128, 128], F32)
        CM = sb("cm", [128, 128], F32)
        EIO = sb("eio", [128, E], F32)
        EIO99 = sb("eio99", [128, E], F32)
        KIDX = sb("kidx", [128, KI], F32)
        ONES1F = sb("ones1f", [1, 128], F32)
        EPS = sb("eps", [128, 1], F32)
        SLB = sb("slb", [128, H], F32)
        BKQ = sb("bkq", [128, H], F32)
        ONESB = sb("ones2b", [1, n_tok], BF16)
        BVHI = sb("bvhi_t", [1, d], BF16)
        BVLO = sb("bvlo_t", [1, d], BF16)
        BPHI = sb("bphi_t", [1, d], BF16)
        BPLO = sb("bplo_t", [1, d], BF16)
        BJ = sb("bj", [128, H * KI], F32)
        LGT = sb("lgt", [128, E * TT], F32)
        PROBS = [sb(f"probs{t}", [128, E], F32) for t in range(TT)]
        OH = sb("ohot", [128, TT * E], F32)
        WVS = sb("wvs", [128, TT], F32)
        TOKE = sb("toke", [128, TT], F32)
        SLOTI = sb("sloti", [128, TT], I32)
        PPRE = sb("ppre", [128, E], F32)
        RUNF = sb("runf", [128, E], F32)
        PCB = sb("pcb", [128, E], BF16)
        C1E9 = sb("c1e9", [128, 1], F32)
        TRILS = sb("trils", [128, 128], BF16)
        THRB = sb("thrb", [128, E], F32)
        THR8 = sb("thr8", [1, E], F32)
        PR = sb("pr", [128, PRW], F32)
        CMPB = sb("cmpb", [128, PRW], BF16)
        GSC = sb("gsc", [128, 9 * E], F32)
        LO = sb("lo", [128, 1], F32)
        HI = sb("hi", [128, 1], F32)
        MID = sb("mid", [128, 1], F32)
        DD1 = sb("dd1", [128, 1], F32)
        TCNT = sb("tcnt", [128, 32], F32)
        TCT = sb("tct", [128, 32], F32)
        CNT2 = sb("cnt2", [128, 2], F32)
        GEB = sb("geb", [128, 2], F32)
        TG = sb("tg", [128, 32], F32)
        TGT = sb("tgt", [128, 32], F32)
        WGB = sb("wgb", [128, KC * E], BF16)
        if PRW >= d:
            SPH, spk = PR.ap()[0:1, 0:d], "pr"
        else:
            _spt = sb("spt", [1, d], F32)
            SPH, spk = _spt.ap()[0:1, :], "spt"
        QW = min(512, n_tok)
        if PRW >= 2 * QW and PRW >= 528 + QW // 2:
            prb = PR.ap().bitcast(BF16)
            P1Q = prb[32:48, 0:QW]
            P2Q = prb[32:48, QW:2 * QW]
            P3Q = prb[64:80, 1056:1056 + QW]
            RQ = PR.ap()[0:16, 0:QW]
            RQ2 = PR.ap()[0:16, QW:2 * QW]
            NSL = PR.ap()[64:65, 0:H]
            IOTAR = PR.ap()[64:65, 16:16 + QW]
        else:
            P1Q = sb("p1q", [16, QW], BF16).ap()[0:16, :]
            P2Q = sb("p2q", [16, QW], BF16).ap()[0:16, :]
            P3Q = sb("p3q", [16, QW], BF16).ap()[0:16, :]
            RQ = sb("rqt", [16, QW], F32).ap()[0:16, :]
            RQ2 = sb("rq2t", [16, QW], F32).ap()[0:16, :]
            NSL = sb("nslt", [1, H], F32).ap()[0:1, :]
            IOTAR = sb("iotat", [1, QW], F32).ap()[0:1, :]
        from contextlib import ExitStack as _ES2
        mid = _ES2()
        XNT = [mid.enter_context(nc.sbuf_tensor(f"xnt{k}", [128, n_tok], BF16))
               for k in range(KC)]
        _kqw0 = mid.enter_context(nc.sbuf_tensor("kqw0", [128, KC * 128], BF16))
        _wvb0 = mid.enter_context(nc.sbuf_tensor("wvb0", [128, KC * SW], BF16))
        KTA = [mid.enter_context(nc.sbuf_tensor(f"kta{i}", [67, n_tok], BF16))
               for i in range(2)]
        QTA = [mid.enter_context(nc.sbuf_tensor(f"qta{i}", [67, n_tok], BF16))
               for i in range(2)]
        early = _ES2()
        esb = lambda n, sh, dt: early.enter_context(nc.sbuf_tensor(n, sh, dt))
        PRT = esb("prt", [E, n_tok], F32)
        XNTOK = [esb(f"xntok{i}", [128, d], BF16) for i in range(4)]
        SM = [esb(f"sm{i}", [128, 8], F32) for i in range(8)]
        SRT = [esb(f"srt{i}", [128, 8], F32) for i in range(8)]
        SQB = [esb("sqb1", [128, d], F32)]

        for apd, dst, key in ((c_idb, IDB, "idb"), (c_idf, IDF, "idf"),
                              (c_cm, CM, "cm"), (c_eio, EIO, "eio"),
                              (c_eio99, EIO99, "eio99"), (c_kidx, KIDX, "kidx"),
                              (c_ones1f, ONES1F, "ones1f"), (c_eps, EPS, "eps"),
                              (c_slb, SLB, "slb"), (c_bkq, BKQ, "bkq"),
                              (c_ones2b, ONESB, "ones2b"),
                              (c_trils, TRILS, "trils")):
            P.op("q", lambda e, a=apd, t=dst: e.dma_start(out=t[:], in_=a),
                 writes=[key], kind="dma")
        for tt in range(TT):
            P.op("q" if tt % 2 == 0 else "g", lambda e, tt=tt: e.dma_start(
                out=ACC[tt][:], in_=x_in[tt * 128:(tt + 1) * 128, :]),
                writes=[f"acc{tt}"], kind="dma")

        P.op("g", lambda e: e.dma_start(
            out=_wvb0.ap().rearrange("p (k f) -> p k f", f=SW),
            in_=bass.AP(wkqv, 2 * d + 0 * SW,
                        [[3 * d, 128], [3 * d * 128, KC], [1, SW]])),
            reads=["wvb0"], writes=["wvb0"], kind="dma")
        for col, base in ((0, 0 * dh), (1, d + 0 * dh)):
            P.op("g", lambda e, base=base, col=col: e.dma_start(
                out=bass.AP(_kqw0, col * dh,
                            [[KC * 128, 128], [128, KC], [1, dh]]),
                in_=bass.AP(wkqv, base, [[3 * d, 128], [3 * d * 128, KC],
                                         [1, dh]])),
                reads=["kqw0"], writes=["kqw0", "kqw1"], kind="dma")

        P.op("v", lambda e: e.tensor_scalar(BKQ[64:128, :], BKQ[64:128, :],
                                            0.125, None, ALU.mult),
             reads=["bkq"], writes=["bkq"])
        for h in range(H):
            P.op("v", lambda e, h=h: e.tensor_scalar(
                BJ[:, h * KI:(h + 1) * KI], KIDX[:], SLB[:, h:h + 1], None,
                ALU.mult), reads=["kidx", "slb"], writes=[f"bj{h}"])

        def split2(row_ap, ohi, olo, hik, lok):
            P.op("q", lambda e: e.dma_start(out=STG[0][0:1, 0:d], in_=row_ap),
                 writes=["stg0"], kind="dma")
            P.op("v", lambda e: e.tensor_copy(out=ohi[0:1, :], in_=STG[0][0:1, 0:d]),
                 reads=["stg0"], writes=[hik])
            P.op("v", lambda e: e.tensor_copy(out=SPH, in_=ohi[0:1, :]),
                 reads=[hik, spk], writes=[spk]),
            P.op("v", lambda e: e.tensor_tensor(STG[0][0:1, 0:d], STG[0][0:1, 0:d],
                                                SPH, ALU.subtract),
                 reads=["stg0", spk], writes=["stg0"])
            P.op("v", lambda e: e.tensor_copy(out=olo[0:1, :], in_=STG[0][0:1, 0:d]),
                 reads=["stg0"], writes=[lok])
        split2(c_bv, BVHI, BVLO, "bvhi", "bvlo")
        split2(c_bp, BPHI, BPLO, "bphi", "bplo")

        P.op("g", lambda e: e.dma_start(
            out=WGB.ap().rearrange("p (k f) -> p k f", f=E),
            in_=bass.AP(wg, 0, [[E, 128], [E * 128, KC], [1, E]])),
            writes=["wgb"], kind="dma")

        psA = pt("psA", [128, 512])
        psB = pt("psB", [128, 512])
        psC = pt("psC", [128, 512])
        psD = pt("psD", [128, 512])
        psS = [pt(f"psS{i}", [128, 512]) for i in range(2)]
        def emit_p123():
          P.op("v", lambda e: e.tensor_scalar(NSL, SLB[0:1, :], -1.0, None,
                                              ALU.mult), reads=["slb", "pr"],
               writes=["pr"])
          for qq in range(n_tok // QW):
            P.op("q", lambda e, qq=qq: e.dma_start(
                out=IOTAR, in_=c_iota[:, qq * QW:(qq + 1) * QW]),
                reads=["pr"], writes=["pr"], kind="dma")
            P.op("t", lambda e, qq=qq: e.matmul(
                psA[0:H, 0:QW], NSL, IOTAR, start=True, stop=True),
                reads=["pr"], writes=["psA"])
            P.op("v", lambda e: e.tensor_copy(out=P1Q[0:H, :],
                                              in_=psA[0:H, 0:QW]),
                 reads=["psA", "pr"], writes=["pr"])
            P.op("v", lambda e: e.tensor_copy(out=RQ[0:H, :],
                                              in_=P1Q[0:H, :]),
                 reads=["pr"], writes=["pr"])
            P.op("v", lambda e: e.tensor_tensor(RQ[0:H, :], psA[0:H, 0:QW],
                                                RQ[0:H, :], ALU.subtract),
                 reads=["psA", "pr"], writes=["pr"])
            P.op("v", lambda e: e.tensor_copy(out=P2Q[0:H, :],
                                              in_=RQ[0:H, :]),
                 reads=["pr"], writes=["pr"])
            P.op("v", lambda e: e.tensor_copy(out=RQ2[0:H, :],
                                              in_=P2Q[0:H, :]),
                 reads=["pr"], writes=["pr"])
            P.op("v", lambda e: e.tensor_tensor(RQ2[0:H, :], RQ[0:H, :],
                                                RQ2[0:H, :], ALU.subtract),
                 reads=["pr"], writes=["pr"])
            P.op("v", lambda e: e.tensor_copy(out=P3Q[0:H, :],
                                              in_=RQ2[0:H, :]),
                 reads=["pr"], writes=["pr"])
            for part, t in ((0, P1Q), (1, P2Q), (2, P3Q)):
                P.op("g", lambda e, part=part, t=t, qq=qq: e.dma_start(
                    out=bass.AP(p123d, part * H * n_tok + qq * QW,
                                [[n_tok, H], [1, QW]]),
                    in_=t[0:H, :]),
                    reads=["pr"], writes=["p123d"], kind="dma")

        from collections import deque as _dq
        p123_fill = _dq(collect(emit_p123))

        NPASS = (QB + 1) // 2
        def emit_kq0_pass(ps):
            qbs = [qb for qb in (2 * ps, 2 * ps + 1) if qb < QB]
            for kc in range(KC):
                for j, qb in enumerate(qbs):
                    P.op("t", lambda e, kc=kc, qb=qb, j=j:
                         e.matmul(psS[j][:, 0:QBW],
                                  _kqw0[:, kc * 128:(kc + 1) * 128],
                                  XNT[kc][:, qb * QBW:(qb + 1) * QBW],
                                  start=(kc == 0), stop=(kc == KC - 1)),
                         reads=["kqw0"] +
                               [f"xnt{kc}:{t}" for t in
                                range(qb * QBW // 128, (qb + 1) * QBW // 128)],
                         writes=[f"psS{j}"])
            for j, qb in enumerate(qbs):
                P.op("s", lambda e, j=j, qb=qb:
                     e.activation(KTA[0][0:64, qb * QBW:(qb + 1) * QBW],
                                  psS[j][0:64, 0:QBW], AF.Identity,
                                  bias=BKQ[0:64, 0:1], scale=1.0),
                     reads=[f"psS{j}", "bkq", "kta0"],
                     writes=["kta0"])
                P.op("s", lambda e, j=j, qb=qb:
                     e.activation(QTA[0][0:64, qb * QBW:(qb + 1) * QBW],
                                  psS[j][64:128, 0:QBW], AF.Identity,
                                  bias=BKQ[64:128, 0:1], scale=0.125),
                     reads=[f"psS{j}", "bkq", "qta0"],
                     writes=["qta0"])
        def emit_kq0_aug():
            P.op("g", lambda e: e.dma_start(
                out=QTA[0][64:67, :],
                in_=bass.AP(p123d, 0, [[H * n_tok, 3], [1, n_tok]])),
                reads=["p123d", "qta0"], writes=["qta0"], kind="dma")

        GRP = min(8, TT)
        kq0_done = 0
        for g0 in range(0, TT, GRP):
            gtiles = list(range(g0, min(TT, g0 + GRP)))
            for tt in gtiles:
                a = f"acc{tt}"
                sm = SM[tt % 8]
                smk = f"sm{tt % 8}"
                sq = STG[0] if tt % 2 == 0 else SQB[0]
                sqk = "stg0" if tt % 2 == 0 else "sqb1"
                P.op("s", lambda e, tt=tt, sq=sq, sm=sm: e.activation(
                    sq[:, 0:d], ACC[tt][:], AF.Square,
                    accum_out=sm[:, 1:2]),
                    reads=[a, smk], writes=[sqk, smk])
                P.op("v", lambda e, tt=tt, sm=sm: e.tensor_reduce(
                    sm[:, 0:1], ACC[tt][:], AX.X, ALU.add),
                    reads=[a, smk], writes=[smk])
                P.op("v", lambda e, sm=sm: e.tensor_scalar(
                    sm[:, 2:3], sm[:, 0:1], 1.0 / d, None, ALU.mult),
                    reads=[smk], writes=[smk])
                P.op("v", lambda e, sm=sm: e.tensor_tensor(
                    sm[:, 3:4], sm[:, 2:3], sm[:, 2:3], ALU.mult),
                    reads=[smk], writes=[smk])
                P.op("v", lambda e, sm=sm: e.scalar_tensor_tensor(
                    sm[:, 4:5], sm[:, 1:2], 1.0 / d, sm[:, 3:4], ALU.mult,
                    ALU.subtract),
                    reads=[smk], writes=[smk])
            for tt in gtiles:
                sm = SM[tt % 8]
                smk = f"sm{tt % 8}"
                P.op("s", lambda e, sm=sm: e.activation(
                    sm[:, 5:6], sm[:, 4:5], AF.Sqrt, bias=EPS[:, 0:1], scale=1.0),
                    reads=[smk, "eps"], writes=[smk])
            for tt in gtiles:
                a = f"acc{tt}"
                sm = SM[tt % 8]
                smk = f"sm{tt % 8}"
                P.op("v", lambda e, sm=sm: e.reciprocal(sm[:, 6:7], sm[:, 5:6]),
                     reads=[smk], writes=[smk])
                P.op("v", lambda e, sm=sm: e.scalar_tensor_tensor(
                    sm[:, 7:8], sm[:, 2:3], -1.0, sm[:, 6:7], ALU.mult, ALU.mult),
                    reads=[smk], writes=[smk])
                xn = XNTOK[tt % 4]
                xnk = f"xntok{tt % 4}"
                P.op("s", lambda e, tt=tt, sm=sm, xn=xn: e.activation(
                    xn[:], ACC[tt][:], AF.Identity, bias=sm[:, 7:8],
                    scale=sm[:, 6:7]),
                    reads=[a, smk], writes=[xnk])
                P.op("q", lambda e, tt=tt, xn=xn: e.dma_start(
                    out=xnd.ap()[tt * 128:(tt + 1) * 128, :], in_=xn[:]),
                    reads=[xnk], writes=["xnd"], kind="dma")
                psT, ptk = (psA, "psA") if tt % 2 == 0 else (psD, "psD")
                pab = psT.ap().bitcast(BF16)
                for j in range(KC):
                    P.op("t", lambda e, j=j, xn=xn, pab=pab: e.transpose(
                        pab[:, j * 128:(j + 1) * 128], xn[:, j * 128:(j + 1) * 128],
                        IDB[:]), reads=[xnk, "idb"], writes=[ptk])
                for j in range(KC):
                    P.op("v", lambda e, j=j, tt=tt, pab=pab: e.tensor_copy(
                        out=XNT[j][:, tt * 128:(tt + 1) * 128],
                        in_=pab[:, j * 128:(j + 1) * 128]),
                        reads=[ptk], writes=[f"xnt{j}:{tt}"])
                for kc in range(KC):
                    P.op("t", lambda e, kc=kc, tt=tt: e.matmul(
                        psB[:, 0:E], XNT[kc][:, tt * 128:(tt + 1) * 128],
                        WGB[:, kc * E:(kc + 1) * E], start=(kc == 0),
                        stop=(kc == KC - 1)),
                        reads=[f"xnt{kc}:{tt}", "wgb"], writes=["psB"])
                P.op("v", lambda e, tt=tt: e.tensor_copy(
                    out=LGT[:, tt * E:(tt + 1) * E], in_=psB[:, 0:E]),
                    reads=["psB"], writes=[f"lgt{tt}"])
                srt = SRT[tt % 8]
                sk = f"srt{tt % 8}"
                P.op("v", lambda e, srt=srt, tt=tt: e.tensor_reduce(
                    srt[:, 0:1], LGT[:, tt * E:(tt + 1) * E], AX.X, ALU.max),
                    reads=[f"lgt{tt}"], writes=[sk])
                P.op("v", lambda e, srt=srt: e.tensor_scalar(
                    srt[:, 1:2], srt[:, 0:1], -1.0, None, ALU.mult),
                    reads=[sk], writes=[sk])
                for _ in range(3):
                    if not p123_fill:
                        break
                    p123_fill.popleft()()
            for tt in gtiles:
                srt = SRT[tt % 8]
                sk = f"srt{tt % 8}"
                P.op("s", lambda e, srt=srt, tt=tt: e.activation(
                    PROBS[tt][:], LGT[:, tt * E:(tt + 1) * E], AF.Exp,
                    bias=srt[:, 1:2], scale=1.0),
                    reads=[f"lgt{tt}", sk], writes=[f"probs{tt}"])
            for tt in gtiles:
                srt = SRT[tt % 8]
                sk = f"srt{tt % 8}"
                P.op("v", lambda e, srt=srt, tt=tt: e.tensor_reduce(
                    srt[:, 2:3], PROBS[tt][:], AX.X, ALU.add),
                    reads=[f"probs{tt}", sk], writes=[sk])
                P.op("v", lambda e, srt=srt: e.reciprocal(srt[:, 3:4], srt[:, 2:3]),
                     reads=[sk], writes=[sk])
                P.op("v", lambda e, srt=srt, tt=tt: e.tensor_scalar(
                    PROBS[tt][:], PROBS[tt][:], srt[:, 3:4], None, ALU.mult),
                    reads=[sk, f"probs{tt}"], writes=[f"probs{tt}"])
                g4 = min(4, TT)
                off = 128 * (tt % g4)
                P.op("t", lambda e, tt=tt, off=off: e.transpose(
                    bass.AP(psC, off, [[512, E], [1, 128]]), PROBS[tt][:], IDF[:]),
                    reads=[f"probs{tt}", "idf"], writes=["psC"])
                if tt % g4 == g4 - 1:
                    q = tt // g4
                    w4 = g4 * 128
                    P.op("v", lambda e, q=q, w4=w4: e.tensor_copy(
                        out=PRT[:, q * w4:(q + 1) * w4],
                        in_=bass.AP(psC, 0, [[512, E], [1, w4]])),
                        reads=["psC"], writes=["prt"])
            while (kq0_done < NPASS and
                   min(TT, g0 + GRP) * 128 >= 2 * (kq0_done + 1) * QBW):
                emit_kq0_pass(kq0_done)
                kq0_done += 1
        while p123_fill:
            p123_fill.popleft()()
        while kq0_done < NPASS:
            emit_kq0_pass(kq0_done)
            kq0_done += 1
        emit_kq0_aug()
        P.op("q", lambda e: e.dma_start(out=cc_in.ap(), in_=PRT[:]),
             reads=["prt"], writes=["cc_in"], kind="dma")

        def emit_collective():
            P.op("g", lambda e: e.collective_compute(
                "AllGather", ALU.bypass, replica_groups=[list(range(NC))],
                ins=[cc_in.ap()], outs=[cc_out.ap()]),
                reads=["cc_in"], writes=["cc_out"], kind="cc")
        P.op("v", lambda e: e.memset(PR[:], 0.0), writes=["pr"])
        P.op("v", lambda e: e.memset(TCNT[:], 0.0), writes=["tcnt"])
        P.op("v", lambda e: e.memset(LO[:], 0.0), writes=["lo"])
        P.op("v", lambda e: e.memset(HI[:], 1.0000001), writes=["hi"])
        P.op("v", lambda e: e.memset(CMPB[:], 0.0), writes=["cmpb"])
        def emit_prd():
            for r in range(NC):
                for hf in range(HFC):
                    P.op("q", lambda e, r=r, hf=hf: e.dma_start(
                        out=bass.AP(prd, (r * HFC + hf) * PRW,
                                    [[PPE * PRW, E], [1, U]]),
                        in_=bass.AP(cc_out, (r * E) * n_tok + hf * U,
                                    [[n_tok, E], [1, U]])),
                        reads=["cc_out"], writes=["prdram"], kind="dma")

        def emit_zero_xgd():
            for blk in range(E * PAD // 128):
                P.op("q", lambda e, blk=blk: e.dma_start(
                    out=xgd.ap()[blk * 128:(blk + 1) * 128, :],
                    in_=c_zeros.ap()),
                    writes=["xgd"], kind="dma")

        def emit_pr_load():
            P.op("q", lambda e: e.dma_start(out=PR[:], in_=prd.ap()),
                 reads=["prdram", "pr", "tcnt", "lo", "hi", "cmpb"],
                 writes=["pr"], kind="dma")

        def emit_search_iters(a, b):
            for _ in range(a, b):
                P.op("v", lambda e: e.tensor_tensor(MID[:], LO[:], HI[:], ALU.add),
                     reads=["lo", "hi"], writes=["mid"])
                P.op("v", lambda e: e.tensor_scalar(MID[:], MID[:], 0.5, None,
                                                    ALU.mult),
                     reads=["mid"], writes=["mid"])
                P.op("v", lambda e: e.tensor_scalar(
                    CMPB[:], PR[:], MID[:, 0:1], None, ALU.is_ge),
                    reads=["pr", "mid"], writes=["cmpb"])
                P.op("v", lambda e: e.tensor_reduce(
                    TCNT[:, 0:1], CMPB[:], AX.X, ALU.add),
                    reads=["cmpb", "tcnt"], writes=["tcnt"])
                P.op("v", lambda e: e.transpose(TCT[:], TCNT[:]),
                     reads=["tcnt"], writes=["tct"])
                P.op("v", lambda e: e.tensor_reduce(
                    CNT2[:], TCT.ap().rearrange("p (g u) -> p g u", u=PPE),
                    AX.X, ALU.add), reads=["tct"], writes=["cnt2"])
                P.op("v", lambda e: e.tensor_scalar(
                    GEB[:], CNT2[:], float(cap), None, ALU.is_ge),
                    reads=["cnt2"], writes=["geb"])
                P.op("v", lambda e: e.tensor_copy(
                    out=TG[:, 0:PPE], in_=GEB[:, 0:1].to_broadcast((128, PPE))),
                    reads=["geb", "tg"], writes=["tg"])
                P.op("v", lambda e: e.tensor_copy(
                    out=TG[:, PPE:2 * PPE],
                    in_=GEB[:, 1:2].to_broadcast((128, PPE))),
                    reads=["geb", "tg"], writes=["tg"])
                P.op("v", lambda e: e.transpose(TGT[:], TG[:]),
                     reads=["tg"], writes=["tgt"])
                P.op("v", lambda e: e.tensor_tensor(DD1[:], MID[:], LO[:],
                                                    ALU.subtract),
                     reads=["mid", "lo"], writes=["dd1"])
                P.op("v", lambda e: e.scalar_tensor_tensor(
                    LO[:], DD1[:], TGT[:, 0:1], LO[:], ALU.mult, ALU.add),
                    reads=["dd1", "tgt", "lo"], writes=["lo"])
                P.op("v", lambda e: e.tensor_tensor(DD1[:], HI[:], MID[:],
                                                    ALU.subtract),
                     reads=["hi", "mid"], writes=["dd1"])
                P.op("v", lambda e: e.scalar_tensor_tensor(
                    HI[:], DD1[:], TGT[:, 0:1], MID[:], ALU.mult, ALU.add),
                    reads=["dd1", "tgt", "mid"], writes=["hi"])

        def emit_gates_part(t0, t1):
            if t0 == 0:
                P.op("g", lambda e: e.dma_start(
                    out=bass.AP(thd, 0, [[1, 128], [1, 1]]), in_=LO[:]),
                    reads=["lo"], writes=["thd"], kind="dma")
                P.op("g", lambda e: e.dma_start(
                    out=THR8[:], in_=bass.AP(thd, 0, [[1, 1], [PPE, E]]),
                    allow_slow_non_contiguous=True),
                    reads=["thd"], writes=["thr8"], kind="dma")
                P.op("t", lambda e: e.matmul(psA[:, 0:E], ONES1F[0:1, :], THR8[:],
                                             start=True, stop=True),
                     reads=["thr8", "ones1f"], writes=["psA"])
                P.op("v", lambda e: e.tensor_copy(out=THRB[:], in_=psA[:, 0:E]),
                     reads=["psA"], writes=["thrb"])
            for tt in range(t0, t1):
                P.op("v", lambda e, tt=tt: e.tensor_tensor(
                    GSC[:, 0:E], PROBS[tt][:], THRB[:], ALU.is_ge),
                    reads=[f"probs{tt}", "thrb", "gsc"], writes=["gsc"])
                P.op("v", lambda e, tt=tt: e.scalar_tensor_tensor(
                    GSC[:, E:2 * E], GSC[:, 0:E], 2.0, PROBS[tt][:],
                    ALU.mult, ALU.add),
                    reads=["gsc", f"probs{tt}"], writes=["gsc"])
                P.op("v", lambda e: e.tensor_reduce(
                    GSC[:, 2 * E:2 * E + 1], GSC[:, E:2 * E], AX.X, ALU.max),
                    reads=["gsc"], writes=["gsc"])
                P.op("v", lambda e: e.tensor_scalar(
                    GSC[:, 3 * E:4 * E], GSC[:, E:2 * E], GSC[:, 2 * E:2 * E + 1],
                    None, ALU.is_ge), reads=["gsc"], writes=["gsc"])
                P.op("v", lambda e: e.tensor_tensor(
                    GSC[:, 4 * E:5 * E], GSC[:, 3 * E:4 * E], EIO99[:], ALU.mult),
                    reads=["gsc", "eio99"], writes=["gsc"])
                P.op("v", lambda e: e.tensor_scalar(
                    GSC[:, 4 * E:5 * E], GSC[:, 4 * E:5 * E], 99.0, None, ALU.add),
                    reads=["gsc"], writes=["gsc"])
                P.op("v", lambda e: e.tensor_reduce(
                    GSC[:, 5 * E:5 * E + 1], GSC[:, 4 * E:5 * E], AX.X, ALU.min),
                    reads=["gsc"], writes=["gsc"])
                P.op("v", lambda e, tt=tt: e.tensor_scalar(
                    OH[:, tt * E:(tt + 1) * E], EIO[:], GSC[:, 5 * E:5 * E + 1],
                    None, ALU.is_equal), reads=["gsc", "eio", "oh"], writes=["oh"])
                P.op("v", lambda e, tt=tt: e.tensor_copy(
                    out=TOKE[:, tt:tt + 1], in_=GSC[:, 5 * E:5 * E + 1]),
                    reads=["gsc", "toke"], writes=["toke"])
                P.op("v", lambda e, tt=tt: e.tensor_tensor(
                    GSC[:, 7 * E:8 * E], OH[:, tt * E:(tt + 1) * E], PROBS[tt][:],
                    ALU.mult), reads=["gsc", "oh", f"probs{tt}"], writes=["gsc"])
                P.op("v", lambda e: e.tensor_reduce(
                    GSC[:, 8 * E:8 * E + 1], GSC[:, 7 * E:8 * E], AX.X, ALU.add),
                    reads=["gsc"], writes=["gsc"])
                P.op("v", lambda e, tt=tt: e.tensor_copy(
                    out=WVS[:, tt:tt + 1], in_=GSC[:, 8 * E:8 * E + 1]),
                    reads=["gsc"], writes=["wvs"])

        early.close()
        with ExitStack() as att:
            asb = lambda n, s, t: att.enter_context(nc.sbuf_tensor(n, s, t))
            apt = lambda n, s: att.enter_context(nc.psum_tensor(n, s, F32))
            YT = [asb(f"yt{k}", [128, n_tok], BF16) for k in range(KC)]
            RECQ = asb("recq", [64, QBW], F32)
            RECR = [asb(f"recr{q}", [1, QBW], F32).ap()[0:1, :] for q in range(QB)]
            psY = apt("psY", [65, 2 * QBW])
            from contextlib import ExitStack as _ES4
            heads_scope = _ES4()
            hsb = lambda n, sh, dt: heads_scope.enter_context(nc.sbuf_tensor(n, sh, dt))
            KQW = [_kqw0, _kqw0]
            V4 = [hsb(f"v4_{i}", [128, TT * HPG * (dh + 1)], BF16) for i in range(2)]
            WVB = [_wvb0, _wvb0]
            PTB = [hsb(f"ptb{i}", [128, QBW], BF16) for i in range(3)]

            for i in range(2):
                P.op("v", lambda e, i=i: e.memset(KTA[i][64:67, :], 1.0),
                     writes=[f"ktaones{i}"])
                P.op("v", lambda e, i=i: e.memset(
                    bass.AP(V4[i], dh, [[TT * HPG * (dh + 1), 128],
                                        [dh + 1, TT * HPG], [1, 1]]), 1.0),
                    writes=[f"v4ones{i}"])

            def stage_wv(g, buf):
                P.op("g", lambda e, g=g: e.dma_start(
                    out=WVB[0].ap().rearrange("p (k f) -> p k f", f=SW),
                    in_=bass.AP(wkqv, 2 * d + g * SW,
                                [[3 * d, 128], [3 * d * 128, KC], [1, SW]])),
                    reads=["wvb0"], writes=["wvb0"], kind="dma")

            def produce_v(g, buf):
                for tt in range(TT):
                    psh, phk = (psA, "psA") if tt % 2 == 0 else (psB, "psB")
                    for kc in range(KC):
                        P.op("t", lambda e, kc=kc, tt=tt, buf=buf, psh=psh: e.matmul(
                            psh[:, 0:SW], XNT[kc][:, tt * 128:(tt + 1) * 128],
                            WVB[buf][:, kc * SW:(kc + 1) * SW],
                            start=(kc == 0), stop=False),
                            reads=[f"xnt{kc}:{tt}", "wvb0"], writes=[phk])
                    P.op("t", lambda e, tt=tt, g=g, psh=psh: e.matmul(
                        psh[:, 0:SW], ONESB[0:1, tt * 128:(tt + 1) * 128],
                        BVHI[0:1, g * SW:(g + 1) * SW], start=False, stop=False),
                        reads=["ones2b", "bvhi"], writes=[phk])
                    P.op("t", lambda e, tt=tt, g=g, psh=psh: e.matmul(
                        psh[:, 0:SW], ONESB[0:1, tt * 128:(tt + 1) * 128],
                        BVLO[0:1, g * SW:(g + 1) * SW], start=False, stop=True),
                        reads=["ones2b", "bvlo"], writes=[phk])
                    dst = bass.AP(V4[buf], tt * HPG * (dh + 1),
                                  [[TT * HPG * (dh + 1), 128], [dh + 1, HPG], [1, dh]])
                    src = psh.ap()[:, 0:SW].rearrange("p (h f) -> p h f", f=dh)
                    P.op("v", lambda e, dst=dst, src=src: e.tensor_copy(
                        out=dst, in_=src),
                        reads=[phk, f"v4_{buf}:{tt}"], writes=[f"v4_{buf}:{tt}"])

            def stage_kq(h, buf):
                for col, base in ((0, h * dh), (1, d + h * dh)):
                    P.op("g", lambda e, base=base, col=col: e.dma_start(
                        out=bass.AP(KQW[0], col * dh,
                                    [[KC * 128, 128], [128, KC], [1, dh]]),
                        in_=bass.AP(wkqv, base, [[3 * d, 128], [3 * d * 128, KC],
                                                 [1, dh]])),
                        reads=["kqw0"], writes=["kqw0", "kqw1"], kind="dma")

            def produce_kq(h, buf):
                npass = (QB + 1) // 2
                for ps in range(npass):
                    qbs = [qb for qb in (2 * ps, 2 * ps + 1) if qb < QB]
                    for kc in range(KC):
                        for j, qb in enumerate(qbs):
                            P.op("t", lambda e, kc=kc, qb=qb, j=j, buf=buf:
                                 e.matmul(psS[j][:, 0:QBW],
                                          KQW[buf][:, kc * 128:(kc + 1) * 128],
                                          XNT[kc][:, qb * QBW:(qb + 1) * QBW],
                                          start=(kc == 0), stop=(kc == KC - 1)),
                                 reads=["kqw0"] +
                                       [f"xnt{kc}:{t}" for t in
                                        range(qb * QBW // 128, (qb + 1) * QBW // 128)],
                                 writes=[f"psS{j}"])
                    for j, qb in enumerate(qbs):
                        P.op("s", lambda e, j=j, qb=qb, buf=buf, h=h:
                             e.activation(KTA[buf][0:64, qb * QBW:(qb + 1) * QBW],
                                          psS[j][0:64, 0:QBW], AF.Identity,
                                          bias=BKQ[0:64, h:h + 1], scale=1.0),
                             reads=[f"psS{j}", "bkq", f"kta{buf}"],
                             writes=[f"kta{buf}"])
                        P.op("s", lambda e, j=j, qb=qb, buf=buf, h=h:
                             e.activation(QTA[buf][0:64, qb * QBW:(qb + 1) * QBW],
                                          psS[j][64:128, 0:QBW], AF.Identity,
                                          bias=BKQ[64:128, h:h + 1], scale=0.125),
                             reads=[f"psS{j}", "bkq", f"qta{buf}"],
                             writes=[f"qta{buf}"])
                P.op("g", lambda e, h=h, buf=buf: e.dma_start(
                    out=QTA[buf][64:67, :],
                    in_=bass.AP(p123d, h * n_tok, [[H * n_tok, 3], [1, n_tok]])),
                    reads=["p123d", f"qta{buf}"], writes=[f"qta{buf}"], kind="dma")

            from collections import deque as _dq5
            pending_norms = _dq5()

            def attend(h, buf, vbuf, fill=None):
                gih = h % HPG
                ytk = f"yt{(h * dh) // 128}"
                yro = (h * dh) % 128
                ring = [(psS[0], "psS0"), (psS[1], "psS1"), (psC, "psC"),
                        (psD, "psD")]
                NPTB = len(PTB)
                kta_r = [f"kta{buf}", f"ktaones{buf}"]
                qta_r = [f"qta{buf}"]

                def norm_recip(qb):
                    off = (qb % 2) * QBW
                    P.op("v", lambda e, qb=qb, off=off: e.reciprocal(
                        RECR[qb], psY[64:65, off:off + QBW]),
                        reads=[f"psY{qb % 2}", f"recr{qb}", "xnd"],
                        writes=[f"recr{qb}"])

                def norm_rest(qb, ytk=None, yro=None, h=h):
                    ytk = ytk if ytk is not None else f"yt{(h * dh) // 128}"
                    yro = yro if yro is not None else (h * dh) % 128
                    off = (qb % 2) * QBW
                    P.op("t", lambda e, qb=qb: e.matmul(
                        psB[0:64, 0:QBW], ONES1F[0:1, 0:64], RECR[qb],
                        start=True, stop=True),
                        reads=[f"recr{qb}", "ones1f"], writes=["psB"])
                    P.op("s", lambda e: e.copy(RECQ[0:64, :], psB[0:64, 0:QBW]),
                        reads=["psB", "recq", "xnd"], writes=["recq"])
                    P.op("v", lambda e, qb=qb, off=off, yro=yro, h=h:
                         e.tensor_tensor(
                             YT[(h * dh) // 128][yro:yro + 64,
                                                 qb * QBW:(qb + 1) * QBW],
                             psY[0:64, off:off + QBW],
                             RECQ[0:64, :], ALU.mult),
                        reads=[f"psY{qb % 2}", "recq", ytk, "xnd", "cc_in"],
                        writes=[ytk])

                chunks = []
                for qb in range(QB):
                    lk = ((qb + 1) * QBW - 1) // 128
                    for ki in range(lk + 1):
                        q0 = max(qb * QBW, ki * 128)
                        chunks.append((qb, ki, q0, (qb + 1) * QBW - q0,
                                       ki == lk))
                LAG = 2
                norm_later = []

                def emit_scores(i):
                    qb, ki, q0, wq, is_last = chunks[i]
                    pse, pk = ring[i % 4]
                    P.op("t", lambda e, ki=ki, q0=q0, wq=wq, pse=pse, buf=buf:
                         e.matmul(pse[:, 0:wq],
                                  KTA[buf][0:67, ki * 128:(ki + 1) * 128],
                                  QTA[buf][0:67, q0:q0 + wq],
                                  start=True, stop=True),
                         reads=kta_r + qta_r, writes=[pk])
                    if q0 == ki * 128:
                        P.op("v", lambda e, pse=pse: e.tensor_tensor(
                            pse[:, 0:128], pse[:, 0:128], CM[:], ALU.add),
                            reads=["cm", pk], writes=[pk])
                    pb = PTB[i % NPTB]
                    pbk = f"ptb{i % NPTB}"
                    P.op("s", lambda e, pse=pse, pb=pb, ki=ki, wq=wq, h=h:
                         e.activation(pb[:, 0:wq], pse[:, 0:wq], AF.Exp,
                                      bias=BJ[:, h * KI + ki:h * KI + ki + 1],
                                      scale=1.0),
                         reads=[pk, f"bj{h}", pbk], writes=[pbk])

                def emit_av(i):
                    qb, ki, q0, wq, is_last = chunks[i]
                    pb = PTB[i % NPTB]
                    pbk = f"ptb{i % NPTB}"
                    off = (qb % 2) * QBW
                    c0 = off + (q0 - qb * QBW)
                    vsl = bass.AP(V4[vbuf],
                                  ki * HPG * (dh + 1) + gih * (dh + 1),
                                  [[TT * HPG * (dh + 1), 128], [1, dh + 1]])
                    P.op("t", lambda e, vsl=vsl, pb=pb, c0=c0, wq=wq, ki=ki,
                         lk=is_last: e.matmul(
                             psY[0:65, c0:c0 + wq], vsl, pb[:, 0:wq],
                             start=(ki == 0), stop=lk),
                         reads=[f"v4_{vbuf}:{ki}", f"v4ones{vbuf}", pbk],
                         writes=[f"psY{qb % 2}"])
                    if is_last:
                        norm_recip(qb)
                        norm_later.append((qb, i))

                N = len(chunks)
                for i in range(N + LAG):
                    if i < N:
                        emit_scores(i)
                    if i - LAG >= 0:
                        emit_av(i - LAG)
                    while norm_later and norm_later[0][1] <= i - LAG - 6:
                        norm_rest(norm_later.pop(0)[0])
                    if fill:
                        for _ in range(3):
                            if not fill:
                                break
                            fill.popleft()()
                ytk_h, yro_h = ytk, yro
                for qb, _ in norm_later:
                    pending_norms.append(
                        lambda qb=qb: norm_rest(qb, ytk_h, yro_h, h))
            def emit_slots_part(t0, t1):
              if t0 == 0:
                P.op("v", lambda e: e.tensor_reduce(
                  RUNF[:],
                  bass.AP(OH, 0, [[TT * E, 128], [1, E], [E, TT]]),
                  AX.X, ALU.add), reads=["oh"], writes=["runf"])
                P.op("v", lambda e: e.tensor_copy(out=PCB[:], in_=RUNF[:]),
                     reads=["runf"], writes=["pcb"])
                P.op("t", lambda e: e.matmul(psB[:, 0:E], TRILS[:], PCB[:],
                                             start=True, stop=True),
                     reads=["trils", "pcb"], writes=["psB"])
                P.op("v", lambda e: e.tensor_copy(out=PPRE[:], in_=psB[:, 0:E]),
                     reads=["psB"], writes=["ppre"])
                P.op("v", lambda e: e.memset(RUNF[:], 0.0),
                     reads=["runf"], writes=["runf"])
                P.op("v", lambda e: e.memset(C1E9[:], 1.0e9), writes=["c1e9"])
              for tt in range(t0, t1):
                  P.op("v", lambda e: e.tensor_tensor(
                      GSC[:, 0:E], PPRE[:], RUNF[:], ALU.add),
                      reads=["ppre", "runf", "gsc"], writes=["gsc"])
                  P.op("v", lambda e, tt=tt: e.tensor_tensor(
                      GSC[:, E:2 * E], GSC[:, 0:E], OH[:, tt * E:(tt + 1) * E],
                      ALU.mult), reads=["gsc", "oh"], writes=["gsc"])
                  P.op("v", lambda e: e.tensor_reduce(
                      GSC[:, 2 * E:2 * E + 1], GSC[:, E:2 * E], AX.X, ALU.add),
                      reads=["gsc"], writes=["gsc"])
                  P.op("v", lambda e, tt=tt: e.scalar_tensor_tensor(
                      GSC[:, 3 * E:3 * E + 1], TOKE[:, tt:tt + 1], float(PAD),
                      GSC[:, 2 * E:2 * E + 1], ALU.mult, ALU.add),
                      reads=["gsc", "toke"], writes=["gsc"])
                  P.op("v", lambda e: e.tensor_scalar(
                      GSC[:, 4 * E:4 * E + 1], GSC[:, 2 * E:2 * E + 1], float(PAD),
                      None, ALU.is_lt), reads=["gsc"], writes=["gsc"])
                  P.op("v", lambda e: e.tensor_scalar(
                      GSC[:, 5 * E:5 * E + 1], GSC[:, 4 * E:4 * E + 1], -1.0e6,
                      1.0e6, ALU.mult, ALU.add), reads=["gsc"], writes=["gsc"])
                  P.op("v", lambda e: e.scalar_tensor_tensor(
                      GSC[:, 6 * E:6 * E + 1], GSC[:, 3 * E:3 * E + 1],
                      GSC[:, 4 * E:4 * E + 1], GSC[:, 5 * E:5 * E + 1],
                      ALU.mult, ALU.add),
                      reads=["gsc"], writes=["gsc"])
                  P.op("v", lambda e, tt=tt: e.tensor_copy(
                      out=SLOTI[:, tt:tt + 1], in_=GSC[:, 6 * E:6 * E + 1]),
                      reads=["gsc"], writes=["sloti"])
                  P.op("v", lambda e, tt=tt: e.tensor_tensor(
                      RUNF[:], RUNF[:], OH[:, tt * E:(tt + 1) * E], ALU.add),
                      reads=["runf", "oh"], writes=["runf"])
            def emit_scatter():
                if PRW >= d:
                    bounces = [(CMPB.ap()[:, 0:d], ["cmpb"]),
                               (PR.ap().bitcast(BF16)[:, 0:d], ["pr"])]
                else:
                    bounces = [(KQW[0].ap()[:, 0:d], ["kqw0", "kqw1"])]
                    if TT * HPG * (dh + 1) >= d:
                        bounces.append(
                            (bass.AP(V4[0], 0,
                                     [[TT * HPG * (dh + 1), 128], [1, d]]),
                             [f"v4_0:{i}" for i in range(TT)] + ["v4ones0"]))
                for tt in range(TT):
                    xt, xtks = bounces[tt % len(bounces)]
                    P.op("q", lambda e, tt=tt, xt=xt: e.dma_start(
                        out=xt, in_=xnd.ap()[tt * 128:(tt + 1) * 128, :]),
                        reads=["xnd"] + xtks, writes=xtks, kind="dma")
                    P.op("g", lambda e, tt=tt, xt=xt: e.indirect_dma_start(
                        out=xgd.ap(),
                        out_offset=IndirectOffsetOnAxis(SLOTI[:, tt:tt + 1], 0),
                        in_=xt,
                        in_offset=None,
                        bounds_check=E * PAD - 1,
                        oob_is_err=False),
                        reads=["sloti"] + xtks, writes=["xgd"] + xtks, kind="dma")
            produce_v(0, 0)

            from collections import deque
            emitted_iters = 0
            scatter_done = False
            SI = cfg["search_iters"]
            fill = deque()
            for h in range(H):
                g = h // HPG
                buf = h % 2
                vbuf = g % 2
                if h % HPG == 0 and g + 1 < NG:
                    stage_wv(g + 1, (g + 1) % 2)
                if h + 1 < H:
                    stage_kq(h + 1, (h + 1) % 2)
                    produce_kq(h + 1, (h + 1) % 2)
                while pending_norms:
                    pending_norms.popleft()()
                if H >= 14:
                    if 1 <= h <= 8:
                        a0, a1 = (h - 1) * SI // 8, h * SI // 8
                        fill.extend(collect(
                            lambda: emit_search_iters(a0, a1)))
                    elif h == 9:
                        fill.extend(collect(
                            lambda: emit_gates_part(0, TT // 2)))
                    elif h == 10:
                        fill.extend(collect(
                            lambda: emit_gates_part(TT // 2, TT)))
                    elif h == 11:
                        fill.extend(collect(
                            lambda: emit_slots_part(0, TT // 2)))
                    elif h == 12:
                        fill.extend(collect(
                            lambda: emit_slots_part(TT // 2, TT)))
                attend(h, buf, vbuf, fill)
                if h == 0:
                    emit_collective()
                    emit_prd()
                    emit_pr_load()
                    emit_zero_xgd()
                if h % HPG == 0 and g + 1 < NG:
                    produce_v(g + 1, (g + 1) % 2)
                if H >= 14:
                    if h == H - 2 and PRW >= d:
                        emit_scatter()
                        scatter_done = True
                else:
                    if h >= 1 and emitted_iters < SI:
                        step = max(1, SI // max(1, (H - 2)))
                        nxt = min(SI, emitted_iters + step)
                        emit_search_iters(emitted_iters, nxt)
                        emitted_iters = nxt
            while pending_norms:
                pending_norms.popleft()()
            while fill:
                fill.popleft()()
            if H < 14:
                if emitted_iters < SI:
                    emit_search_iters(emitted_iters, SI)
                emit_gates_part(0, TT)
                emit_slots_part(0, TT)
            if not scatter_done:
                emit_scatter()
            heads_scope.close()
            wpb_extra = ["xgd"] if PRW < d else []

            with ExitStack() as prj:
                WPB = [prj.enter_context(nc.sbuf_tensor(f"wpb{k}", [128, d], BF16))
                       for k in range(KC)]
                ytall = [f"yt{k}" for k in range(KC)]
                for kc in range(KC):
                    P.op("g", lambda e, kc=kc: e.dma_start(
                        out=WPB[kc][:],
                        in_=wproj.ap()[kc * 128:(kc + 1) * 128, :]),
                        reads=["p123d"] + ytall + wpb_extra,
                        writes=[f"wpb{kc}"], kind="dma")
                ytkeys = [f"yt{k}" for k in range(KC)]
                for tt in range(TT):
                    for db in range(DBLK):
                        psp, ppk = ((psA, "psA") if (tt * DBLK + db) % 2 == 0
                                    else (psB, "psB"))
                        for kc in range(KC):
                            P.op("t", lambda e, kc=kc, tt=tt, db=db, psp=psp:
                                 e.matmul(psp[:, 0:DW],
                                          YT[kc][:, tt * 128:(tt + 1) * 128],
                                          WPB[kc][:, db * DW:(db + 1) * DW],
                                          start=(kc == 0), stop=False),
                                 reads=ytkeys + [f"wpb{kc}"], writes=[ppk])
                        P.op("t", lambda e, tt=tt, db=db, psp=psp: e.matmul(
                            psp[:, 0:DW], ONESB[0:1, tt * 128:(tt + 1) * 128],
                            BPHI[0:1, db * DW:(db + 1) * DW],
                            start=False, stop=False),
                            reads=["ones2b", "bphi"], writes=[ppk])
                        P.op("t", lambda e, tt=tt, db=db, psp=psp: e.matmul(
                            psp[:, 0:DW], ONESB[0:1, tt * 128:(tt + 1) * 128],
                            BPLO[0:1, db * DW:(db + 1) * DW],
                            start=False, stop=True),
                            reads=["ones2b", "bplo"], writes=[ppk])
                        P.op("v", lambda e, tt=tt, db=db, psp=psp: e.tensor_tensor(
                            ACC[tt][:, db * DW:(db + 1) * DW],
                            ACC[tt][:, db * DW:(db + 1) * DW],
                            psp[:, 0:DW], ALU.add),
                            reads=[ppk, f"acc{tt}"], writes=[f"acc{tt}"])

        mid.close()
        with ExitStack() as ffn:
            fsb = lambda n, sh, t: ffn.enter_context(nc.sbuf_tensor(n, sh, t))
            fpt = lambda n, sh: ffn.enter_context(nc.psum_tensor(n, sh, F32))
            W2B1 = fsb("w2b", [128, KT * d], BF16)
            W1B1 = fsb("w1b", [128, KC * FF], BF16)
            XGT = [fsb(f"xgt{k}", [128, PAD], BF16) for k in range(KC)]
            H1G2 = [[fsb(f"h1g{b}_{k}", [128, PAD], BF16) for k in range(KT)]
                    for b in range(2)]
            XGR = [fsb(f"xgr{i}", [128, d], BF16) for i in range(2)]
            XH2S = [fsb(f"xh2s{i}", [128, d], BF16) for i in range(2)]
            FH = None
            FS = [psS[0], psS[1], psC, psD]
            FSK = ["psS0", "psS1", "psC", "psD"]
            ytall_f = [f"yt{k}" for k in range(KC)]
            accl = f"acc{TT - 1}"

            def stage_w1(e_next, first=False):
                P.op("g", lambda e, e_i=e_next: e.dma_start(
                    out=W1B1.ap().rearrange("p (k f) -> p k f", f=FF),
                    in_=bass.AP(w1, e_i * d * FF,
                                [[FF, 128], [128 * FF, KC], [1, FF]])),
                    reads=["w1b"] + ([accl] if first else []),
                    writes=["w1b"], kind="dma")

            def load_xgr(e_i, q4s, first=False):
                for q4 in q4s:
                    xr = XGR[q4 % 2]
                    xrk = f"xgr{q4 % 2}"
                    P.op("q", lambda e, e_i=e_i, q4=q4, xr=xr: e.dma_start(
                        out=xr[:],
                        in_=xgd.ap()[e_i * PAD + q4 * 128:e_i * PAD + (q4 + 1) * 128, :]),
                        reads=["xgd", xrk] + ([accl] if first and q4 == 0 else []),
                        writes=[xrk], kind="dma")

            stage_w1(0, first=True)
            load_xgr(0, range(min(2, NP4)), first=True)
            for e_i in range(E):
                P.op("g", lambda e, e_i=e_i: e.dma_start(
                    out=W2B1.ap().rearrange("p (k f) -> p k f", f=d),
                    in_=bass.AP(w2, e_i * FF * d,
                                [[d, 128], [128 * d, KT], [1, d]])),
                    reads=["w2b"] + ([accl] if e_i == 0 else []),
                    writes=["w2b"], kind="dma")
                for q4 in range(NP4):
                    if q4 >= 2:
                        load_xgr(e_i, [q4])
                    xr = XGR[q4 % 2]
                    xrk = f"xgr{q4 % 2}"
                    fsb16 = FS[q4 % 2].ap().bitcast(BF16)
                    fsk = FSK[q4 % 2]
                    for j in range(KC):
                        P.op("t", lambda e, j=j, xr=xr, fsb16=fsb16: e.transpose(
                            fsb16[:, j * 128:(j + 1) * 128],
                            xr[:, j * 128:(j + 1) * 128], IDB[:]),
                            reads=[xrk, "idb"], writes=[fsk])
                    for j in range(KC):
                        P.op("v", lambda e, j=j, q4=q4, fsb16=fsb16: e.tensor_copy(
                            out=XGT[j][:, q4 * 128:(q4 + 1) * 128],
                            in_=fsb16[:, j * 128:(j + 1) * 128]),
                            reads=[fsk], writes=[f"xgt{j}"])
                H1G = H1G2[e_i % 2]
                hb = e_i % 2
                for fm in range(KT):
                    j = fm % 2
                    for kc in range(KC):
                        P.op("t", lambda e, fm=fm, kc=kc, j=j:
                             e.matmul(FS[j][:, 0:PAD],
                                      W1B1[:, kc * FF + fm * 128:
                                           kc * FF + (fm + 1) * 128],
                                      XGT[kc][:],
                                      start=(kc == 0), stop=(kc == KC - 1)),
                             reads=["w1b", f"xgt{kc}"], writes=[FSK[j]])
                    P.op("s", lambda e, fm=fm, j=j, H1G=H1G: e.activation(
                        H1G[fm][:], FS[j][:, 0:PAD], AF.Gelu),
                        reads=[FSK[j], f"h1g{hb}_{fm}"], writes=[f"h1g{hb}_{fm}"])
                if e_i + 1 < E:
                    stage_w1(e_i + 1)
                    load_xgr(e_i + 1, range(min(2, NP4)))
                for lt in range(NP4):
                    xo = XH2S[lt % 2]
                    xok = f"xh2s{lt % 2}"
                    for kt in range(KT):
                        for db in range(DBLK):
                            j = 2 + db % 2
                            P.op("t", lambda e, kt=kt, lt=lt, db=db, j=j, H1G=H1G:
                                 e.matmul(FS[j][:, 0:DW],
                                          H1G[kt][:, lt * 128:(lt + 1) * 128],
                                          W2B1[:, kt * d + db * DW:
                                               kt * d + (db + 1) * DW],
                                          start=(kt == 0), stop=(kt == KT - 1)),
                                 reads=[f"h1g{hb}_{kt}", "w2b"], writes=[FSK[j]])
                    for db in range(DBLK):
                        j = 2 + db % 2
                        P.op("s", lambda e, db=db, j=j, xo=xo: e.copy(
                            xo[:, db * DW:(db + 1) * DW], FS[j][:, 0:DW]),
                            reads=[FSK[j], xok], writes=[xok])
                    P.op("g", lambda e, e_i=e_i, lt=lt, xo=xo: e.dma_start(
                        out=xh2d.ap()[e_i * PAD + lt * 128:e_i * PAD + (lt + 1) * 128, :],
                        in_=xo[:]),
                        reads=[xok], writes=["xh2d"], kind="dma")

            FH4 = [XGR[0], XGR[1], XH2S[0], XH2S[1]]
            FHK = ["xgr0", "xgr1", "xh2s0", "xh2s1"]
            for i in range(min(4, TT)):
                P.op("v", lambda e, fh=FH4[i]: e.memset(fh[:], 0.0),
                     reads=[FHK[i]], writes=[FHK[i]])
            for tt in range(TT):
                fh = FH4[tt % 4]
                fhk = FHK[tt % 4]
                P.op("g", lambda e, tt=tt, fh=fh: e.indirect_dma_start(
                    out=fh[:],
                    out_offset=None,
                    in_=xh2d.ap(),
                    in_offset=IndirectOffsetOnAxis(SLOTI[:, tt:tt + 1], 0),
                    bounds_check=E * PAD - 1,
                    oob_is_err=False),
                    reads=["xh2d", "sloti", fhk], writes=[fhk], kind="dma")
                P.op("v", lambda e, tt=tt, fh=fh: e.scalar_tensor_tensor(
                    ACC[tt][:], fh[:], WVS[:, tt:tt + 1], ACC[tt][:],
                    ALU.mult, ALU.add),
                    reads=[fhk, "wvs", f"acc{tt}"], writes=[f"acc{tt}"])
                if tt + 4 < TT:
                    P.op("v", lambda e, fh=fh: e.memset(fh[:], 0.0),
                         reads=[fhk], writes=[fhk])
                P.op("q", lambda e, tt=tt: e.dma_start(
                    out=y_out[tt * 128:(tt + 1) * 128, :], in_=ACC[tt][:]),
                    reads=[f"acc{tt}"], writes=[f"y{tt}"], kind="dma")

        from contextlib import ExitStack as _ES
        semstack = _ES()
        sems = {}
        for n in list(P.cnt):
            sems[n] = semstack.enter_context(nc.semaphore("S_" + n))
        with semstack, nc.Block() as block:
            g_finals = [(f"g{r}", P.cnt[f"g{r}"]) for r in range(P.DMA_RING)
                        if P.cnt[f"g{r}"] > 0]
            q_finals = [(f"q{r}", P.cnt[f"q{r}"]) for r in range(P.DMA_RING)
                        if P.cnt[f"q{r}"] > 0]

            @block.tensor
            def _(e):
                _replay(P.streams["t"], e, sems)

            @block.vector
            def _(e):
                _replay(P.streams["v"], e, sems)

            @block.scalar
            def _(e):
                _replay(P.streams["s"], e, sems)

            @block.sync
            def _(e):
                _replay(P.streams["q"], e, sems)
                for sn, val in q_finals:
                    e.wait_ge(sems[sn], val)

            @block.gpsimd
            def _(e):
                _replay(P.streams["g"], e, sems)
                for sn, val in g_finals:
                    e.wait_ge(sems[sn], val)

    return nc


def make_consts(cfg, slopes, b_kqv, b_proj):
    n_tok = cfg["n_tok"]; d = cfg["d"]; H = cfg["H"]; dh = cfg["dh"]
    E = cfg["E"]; KI = n_tok // 128
    f32 = np.float32
    c = {}
    c["c_idb"] = np.eye(128).astype(ml_dtypes.bfloat16)
    c["c_idf"] = np.eye(128, dtype=f32)
    cm = np.zeros((128, 128), f32)
    pi = np.arange(128)
    cm[pi[:, None] > pi[None, :]] = NEG
    c["c_cm"] = cm
    c["c_eio"] = np.ascontiguousarray(
        np.broadcast_to(np.arange(E, dtype=f32), (128, E)))
    c["c_eio99"] = np.ascontiguousarray(c["c_eio"] - 99.0).astype(f32)
    c["c_kidx"] = np.ascontiguousarray(
        (pi[:, None] + 128 * np.arange(KI)[None, :]).astype(f32))
    c["c_iota"] = np.ascontiguousarray(np.arange(n_tok, dtype=f32)[None, :])
    c["c_ones1f"] = np.ones((1, 128), f32)
    c["c_eps"] = np.full((128, 1), 1e-5, f32)
    c["c_slb"] = np.ascontiguousarray(
        np.broadcast_to(np.asarray(slopes, f32)[None, :], (128, H)))
    bkq = np.zeros((128, H), f32)
    bk = np.asarray(b_kqv, f32)
    for h in range(H):
        bkq[0:dh, h] = bk[h * dh:(h + 1) * dh]
        bkq[64:64 + dh, h] = bk[d + h * dh:d + (h + 1) * dh]
    c["c_bkq"] = bkq
    c["c_bv"] = np.ascontiguousarray(bk[2 * d:3 * d][None, :])
    c["c_bp"] = np.ascontiguousarray(np.asarray(b_proj, f32)[None, :])
    c["c_ones2b"] = np.ones((1, n_tok)).astype(ml_dtypes.bfloat16)
    trils = (pi[:, None] < pi[None, :]).astype(np.float32)
    c["c_trils"] = trils.astype(ml_dtypes.bfloat16)
    c["c_zeros"] = np.zeros((128, d), dtype=ml_dtypes.bfloat16)
    return c


def make_in_maps(cfg, x, wkqv, b_kqv, wproj, b_proj, slopes, wgm, w1m, w2m):
    NC = cfg["n_cores"]
    consts = make_consts(cfg, slopes, b_kqv, b_proj)
    shared = dict(
        wkqv=np.ascontiguousarray(np.asarray(wkqv, np.float32)),
        wproj=np.ascontiguousarray(np.asarray(wproj, np.float32)),
        wg=np.ascontiguousarray(np.asarray(wgm, np.float32)),
        w1=np.ascontiguousarray(np.asarray(w1m, np.float32)),
        w2=np.ascontiguousarray(np.asarray(w2m, np.float32)),
        **consts)
    x = np.asarray(x, np.float32)
    return [dict(shared, x=np.ascontiguousarray(x[c])) for c in range(NC)]


def kernel(L0_attn_x, L0_attn_W_kqv, L0_attn_b_kqv, L0_attn_W_proj,
           L0_attn_b_proj, L0_attn_slopes, L0_ffn_Wg, L0_ffn_W1, L0_ffn_W2):
    cfg = full_cfg()
    NC = cfg["n_cores"]
    in_maps = make_in_maps(cfg, L0_attn_x, L0_attn_W_kqv, L0_attn_b_kqv,
                           L0_attn_W_proj, L0_attn_b_proj, L0_attn_slopes,
                           L0_ffn_Wg, L0_ffn_W1, L0_ffn_W2)
    nc = build_program(cfg)
    res = run_bass_kernel_spmd(nc, in_maps, core_ids=list(range(NC)))
    out = np.stack([np.asarray(res.results[c]["y"], np.float32)
                    for c in range(NC)], axis=0)
    return out


# version 41 (final)
# speedup vs baseline: 1.0257x; 1.0257x over previous
import numpy as np
import ml_dtypes
import concourse.bass as bass
import concourse.mybir as mybir
from concourse.bass_core_rust import IndirectOffsetOnAxis
from concourse.bass_utils import run_bass_kernel_spmd

F32 = mybir.dt.float32
I32 = mybir.dt.int32
BF16 = mybir.dt.bfloat16
AF = mybir.ActivationFunctionType
ALU = mybir.AluOpType
AX = mybir.AxisListType
NEG = -1.0e30


def full_cfg():
    return dict(n_cores=8, n_tok=2048, d=1024, H=16, dh=64, FF=2048, E=8,
                search_iters=32)


class Prog:

    DMA_RING = 4

    def __init__(self):
        self.streams = {n: [] for n in ("t", "v", "s", "q", "g")}
        self.cnt = {}
        for n in ("t", "v", "s", "cc"):
            self.cnt[n] = 0
        for qn in ("q", "g"):
            for r in range(self.DMA_RING):
                self.cnt[f"{qn}{r}"] = 0
        self.dma_i = {"q": 0, "g": 0}
        self.state = {}

    def op(self, st, emit, reads=(), writes=(), kind="compute"):
        waits = {}

        def need(sn, val):
            if val > waits.get(sn, 0):
                waits[sn] = val

        if kind == "dma":
            i = self.dma_i[st]
            self.dma_i[st] = i + 1
            sem, inc = f"{st}{i % self.DMA_RING}", 16
            if i >= self.DMA_RING:
                need(sem, self.cnt[sem])
        elif kind == "cc":
            sem, inc = "cc", 1
        else:
            sem, inc = st, 1

        for k in reads:
            w = self.state.get(k)
            if w and w[0] is not None:
                need(*w[0])
        for k in writes:
            w = self.state.get(k)
            if w:
                if w[0] is not None:
                    need(*w[0])
                for sn, val in w[1].items():
                    need(sn, val)
        if st == "t" and kind == "compute":
            waits.pop("t", None)
        self.cnt[sem] += inc
        tag = (sem, self.cnt[sem])
        self.streams[st].append((waits, emit, sem, inc))
        for k in writes:
            self.state[k] = [tag, {}]
        for k in reads:
            w = self.state.setdefault(k, [None, {}])
            w[1][tag[0]] = tag[1]
        return tag


def _replay(ops, eng, sems):
    emitted = {}
    for waits, emit, sem, inc in ops:
        for sn in sorted(waits):
            val = waits[sn]
            if emitted.get(sn, 0) < val:
                eng.wait_ge(sems[sn], val)
                emitted[sn] = val
        emit(eng).then_inc(sems[sem], inc)


def build_program(cfg):
    n_tok = cfg["n_tok"]; d = cfg["d"]; H = cfg["H"]; dh = cfg["dh"]
    FF = cfg["FF"]; E = cfg["E"]; NC = cfg["n_cores"]
    TT = n_tok // 128
    KC = d // 128
    KT = FF // 128
    KI = n_tok // 128
    QBW = min(512, n_tok)
    QB = n_tok // QBW
    T_glob = NC * n_tok
    cap = T_glob // E
    PPE = 128 // E
    HFC = PPE // NC
    U = n_tok // HFC
    PRW = (T_glob * E) // 128
    HPG = min(4, H)
    NG = H // HPG
    DBLK = max(1, d // 512)
    DW = min(512, d)
    TC = min(512, n_tok)
    NTC = n_tok // TC
    SW = HPG * dh
    PAD = max(128, (3 * n_tok) // 16)
    NP4 = PAD // 128

    nc = bass.Bass("TRN2", target_bir_lowering=False)

    ap = lambda t: t.ap()
    x_in = ap(nc.dram_tensor("x", [n_tok, d], F32, kind="ExternalInput"))
    wkqv = nc.dram_tensor("wkqv", [d, 3 * d], F32, kind="ExternalInput")
    wproj = nc.dram_tensor("wproj", [d, d], F32, kind="ExternalInput")
    wg = nc.dram_tensor("wg", [d, E], F32, kind="ExternalInput")
    w1 = nc.dram_tensor("w1", [E, d, FF], F32, kind="ExternalInput")
    w2 = nc.dram_tensor("w2", [E, FF, d], F32, kind="ExternalInput")
    c_idb = ap(nc.dram_tensor("c_idb", [128, 128], BF16, kind="ExternalInput"))
    c_idf = ap(nc.dram_tensor("c_idf", [128, 128], F32, kind="ExternalInput"))
    c_cm = ap(nc.dram_tensor("c_cm", [128, 128], F32, kind="ExternalInput"))
    c_eio = ap(nc.dram_tensor("c_eio", [128, E], F32, kind="ExternalInput"))
    c_eio99 = ap(nc.dram_tensor("c_eio99", [128, E], F32, kind="ExternalInput"))
    c_kidx = ap(nc.dram_tensor("c_kidx", [128, KI], F32, kind="ExternalInput"))
    c_iota = ap(nc.dram_tensor("c_iota", [1, n_tok], F32, kind="ExternalInput"))
    c_ones1f = ap(nc.dram_tensor("c_ones1f", [1, 128], F32, kind="ExternalInput"))
    c_eps = ap(nc.dram_tensor("c_eps", [128, 1], F32, kind="ExternalInput"))
    c_slb = ap(nc.dram_tensor("c_slb", [128, H], F32, kind="ExternalInput"))
    c_bkq = ap(nc.dram_tensor("c_bkq", [128, H], F32, kind="ExternalInput"))
    c_bv = ap(nc.dram_tensor("c_bv", [1, d], F32, kind="ExternalInput"))
    c_bp = ap(nc.dram_tensor("c_bp", [1, d], F32, kind="ExternalInput"))
    c_ones2b = ap(nc.dram_tensor("c_ones2b", [1, n_tok], BF16, kind="ExternalInput"))
    c_trils = ap(nc.dram_tensor("c_trils", [128, 128], BF16, kind="ExternalInput"))
    c_zeros = nc.dram_tensor("c_zeros", [128, d], BF16, kind="ExternalInput")
    y_out = ap(nc.dram_tensor("y", [n_tok, d], F32, kind="ExternalOutput"))

    p123d = nc.dram_tensor("p123d", [3, H * n_tok], BF16, kind="Internal")
    cc_in = nc.dram_tensor("cc_in", [E, n_tok], F32, kind="Internal")
    cc_out = nc.dram_tensor("cc_out", [NC * E, n_tok], F32, kind="Internal",
                            addr_space="Shared")
    thd = nc.dram_tensor("thd", [1, 128], F32, kind="Internal")
    prd = nc.dram_tensor("prd", [128, (NC * n_tok * E) // 128], F32, kind="Internal")
    xnd = nc.dram_tensor("xnd", [n_tok, d], BF16, kind="Internal")
    xgd = nc.dram_tensor("xgd", [E * PAD, d], BF16, kind="Internal")
    xh2d = nc.dram_tensor("xh2d", [E * PAD, d], BF16, kind="Internal")

    P = Prog()

    def collect(fn):
        ops = []
        real_op = P.op
        P.op = lambda *a, **k: ops.append((a, k))
        try:
            fn()
        finally:
            del P.op
        return [(lambda a=a, k=k: real_op(*a, **k)) for (a, k) in ops]
    from contextlib import ExitStack
    stack = ExitStack()
    sb = lambda name, shape, dt: stack.enter_context(nc.sbuf_tensor(name, shape, dt))
    pt = lambda name, shape: stack.enter_context(nc.psum_tensor(name, shape, F32))

    with stack:
        ACC = [sb(f"acc{t}", [128, d], F32) for t in range(TT)]
        STG = [sb("stg0", [128, 1024], F32)]
        IDB = sb("idb", [128, 128], BF16)
        IDF = sb("idf", [128, 128], F32)
        CM = sb("cm", [128, 128], F32)
        EIO = sb("eio", [128, E], F32)
        EIO99 = sb("eio99", [128, E], F32)
        KIDX = sb("kidx", [128, KI], F32)
        ONES1F = sb("ones1f", [1, 128], F32)
        EPS = sb("eps", [128, 1], F32)
        SLB = sb("slb", [128, H], F32)
        BKQ = sb("bkq", [128, H], F32)
        ONESB = sb("ones2b", [1, n_tok], BF16)
        BVHI = sb("bvhi_t", [1, d], BF16)
        BVLO = sb("bvlo_t", [1, d], BF16)
        BPHI = sb("bphi_t", [1, d], BF16)
        BPLO = sb("bplo_t", [1, d], BF16)
        BJ = sb("bj", [128, H * KI], F32)
        LGT = sb("lgt", [128, E * TT], F32)
        PROBS = [sb(f"probs{t}", [128, E], F32) for t in range(TT)]
        OH = sb("ohot", [128, TT * E], F32)
        WVS = sb("wvs", [128, TT], F32)
        TOKE = sb("toke", [128, TT], F32)
        SLOTI = sb("sloti", [128, TT], I32)
        PPRE = sb("ppre", [128, E], F32)
        RUNF = sb("runf", [128, E], F32)
        PCB = sb("pcb", [128, E], BF16)
        C1E9 = sb("c1e9", [128, 1], F32)
        TRILS = sb("trils", [128, 128], BF16)
        THRB = sb("thrb", [128, E], F32)
        THR8 = sb("thr8", [1, E], F32)
        PR = sb("pr", [128, PRW], F32)
        CMPB = sb("cmpb", [128, PRW], BF16)
        GSC = sb("gsc", [128, 9 * E], F32)
        LO = sb("lo", [128, 1], F32)
        HI = sb("hi", [128, 1], F32)
        MID = sb("mid", [128, 1], F32)
        DD1 = sb("dd1", [128, 1], F32)
        TCNT = sb("tcnt", [128, 32], F32)
        TCT = sb("tct", [128, 32], F32)
        CNT2 = sb("cnt2", [128, 2], F32)
        GEB = sb("geb", [128, 2], F32)
        TG = sb("tg", [128, 32], F32)
        TGT = sb("tgt", [128, 32], F32)
        WGB = sb("wgb", [128, KC * E], BF16)
        if PRW >= d:
            SPH, spk = PR.ap()[0:1, 0:d], "pr"
        else:
            _spt = sb("spt", [1, d], F32)
            SPH, spk = _spt.ap()[0:1, :], "spt"
        QW = min(512, n_tok)
        if PRW >= 2 * QW and PRW >= 528 + QW // 2:
            prb = PR.ap().bitcast(BF16)
            P1Q = prb[32:48, 0:QW]
            P2Q = prb[32:48, QW:2 * QW]
            P3Q = prb[64:80, 1056:1056 + QW]
            RQ = PR.ap()[0:16, 0:QW]
            RQ2 = PR.ap()[0:16, QW:2 * QW]
            NSL = PR.ap()[64:65, 0:H]
            IOTAR = PR.ap()[64:65, 16:16 + QW]
        else:
            P1Q = sb("p1q", [16, QW], BF16).ap()[0:16, :]
            P2Q = sb("p2q", [16, QW], BF16).ap()[0:16, :]
            P3Q = sb("p3q", [16, QW], BF16).ap()[0:16, :]
            RQ = sb("rqt", [16, QW], F32).ap()[0:16, :]
            RQ2 = sb("rq2t", [16, QW], F32).ap()[0:16, :]
            NSL = sb("nslt", [1, H], F32).ap()[0:1, :]
            IOTAR = sb("iotat", [1, QW], F32).ap()[0:1, :]
        from contextlib import ExitStack as _ES2
        mid = _ES2()
        XNT = [mid.enter_context(nc.sbuf_tensor(f"xnt{k}", [128, n_tok], BF16))
               for k in range(KC)]
        _kqw0 = mid.enter_context(nc.sbuf_tensor("kqw0", [128, KC * 128], BF16))
        _wvb0 = mid.enter_context(nc.sbuf_tensor("wvb0", [128, KC * SW], BF16))
        KTA = [mid.enter_context(nc.sbuf_tensor(f"kta{i}", [67, n_tok], BF16))
               for i in range(2)]
        QTA = [mid.enter_context(nc.sbuf_tensor(f"qta{i}", [67, n_tok], BF16))
               for i in range(2)]
        early = _ES2()
        esb = lambda n, sh, dt: early.enter_context(nc.sbuf_tensor(n, sh, dt))
        PRT = esb("prt", [E, n_tok], F32)
        XNTOK = [esb(f"xntok{i}", [128, d], BF16) for i in range(4)]
        SM = [esb(f"sm{i}", [128, 8], F32) for i in range(4)]
        SRT = [esb(f"srt{i}", [128, 8], F32) for i in range(4)]
        SQB = [esb("sqb1", [128, d], F32)]

        for apd, dst, key in ((c_idb, IDB, "idb"), (c_idf, IDF, "idf"),
                              (c_cm, CM, "cm"), (c_eio, EIO, "eio"),
                              (c_eio99, EIO99, "eio99"), (c_kidx, KIDX, "kidx"),
                              (c_ones1f, ONES1F, "ones1f"), (c_eps, EPS, "eps"),
                              (c_slb, SLB, "slb"), (c_bkq, BKQ, "bkq"),
                              (c_ones2b, ONESB, "ones2b"),
                              (c_trils, TRILS, "trils")):
            P.op("q", lambda e, a=apd, t=dst: e.dma_start(out=t[:], in_=a),
                 writes=[key], kind="dma")
        for tt in range(TT):
            P.op("q" if tt % 2 == 0 else "g", lambda e, tt=tt: e.dma_start(
                out=ACC[tt][:], in_=x_in[tt * 128:(tt + 1) * 128, :]),
                writes=[f"acc{tt}"], kind="dma")

        P.op("g", lambda e: e.dma_start(
            out=_wvb0.ap().rearrange("p (k f) -> p k f", f=SW),
            in_=bass.AP(wkqv, 2 * d + 0 * SW,
                        [[3 * d, 128], [3 * d * 128, KC], [1, SW]])),
            reads=["wvb0"], writes=["wvb0"], kind="dma")
        for col, base in ((0, 0 * dh), (1, d + 0 * dh)):
            P.op("g", lambda e, base=base, col=col: e.dma_start(
                out=bass.AP(_kqw0, col * dh,
                            [[KC * 128, 128], [128, KC], [1, dh]]),
                in_=bass.AP(wkqv, base, [[3 * d, 128], [3 * d * 128, KC],
                                         [1, dh]])),
                reads=["kqw0"], writes=["kqw0", "kqw1"], kind="dma")

        P.op("v", lambda e: e.tensor_scalar(BKQ[64:128, :], BKQ[64:128, :],
                                            0.125, None, ALU.mult),
             reads=["bkq"], writes=["bkq"])
        for h in range(H):
            P.op("v", lambda e, h=h: e.tensor_scalar(
                BJ[:, h * KI:(h + 1) * KI], KIDX[:], SLB[:, h:h + 1], None,
                ALU.mult), reads=["kidx", "slb"], writes=[f"bj{h}"])

        def split2(row_ap, ohi, olo, hik, lok):
            P.op("q", lambda e: e.dma_start(out=STG[0][0:1, 0:d], in_=row_ap),
                 writes=["stg0"], kind="dma")
            P.op("v", lambda e: e.tensor_copy(out=ohi[0:1, :], in_=STG[0][0:1, 0:d]),
                 reads=["stg0"], writes=[hik])
            P.op("v", lambda e: e.tensor_copy(out=SPH, in_=ohi[0:1, :]),
                 reads=[hik, spk], writes=[spk]),
            P.op("v", lambda e: e.tensor_tensor(STG[0][0:1, 0:d], STG[0][0:1, 0:d],
                                                SPH, ALU.subtract),
                 reads=["stg0", spk], writes=["stg0"])
            P.op("v", lambda e: e.tensor_copy(out=olo[0:1, :], in_=STG[0][0:1, 0:d]),
                 reads=["stg0"], writes=[lok])
        split2(c_bv, BVHI, BVLO, "bvhi", "bvlo")
        split2(c_bp, BPHI, BPLO, "bphi", "bplo")

        P.op("g", lambda e: e.dma_start(
            out=WGB.ap().rearrange("p (k f) -> p k f", f=E),
            in_=bass.AP(wg, 0, [[E, 128], [E * 128, KC], [1, E]])),
            writes=["wgb"], kind="dma")

        psA = pt("psA", [128, 512])
        psB = pt("psB", [128, 512])
        psC = pt("psC", [128, 512])
        psD = pt("psD", [128, 512])
        psS = [pt(f"psS{i}", [128, 512]) for i in range(2)]
        def emit_p123():
          P.op("v", lambda e: e.tensor_scalar(NSL, SLB[0:1, :], -1.0, None,
                                              ALU.mult), reads=["slb", "pr"],
               writes=["pr"])
          for qq in range(n_tok // QW):
            P.op("q", lambda e, qq=qq: e.dma_start(
                out=IOTAR, in_=c_iota[:, qq * QW:(qq + 1) * QW]),
                reads=["pr"], writes=["pr"], kind="dma")
            P.op("t", lambda e, qq=qq: e.matmul(
                psA[0:H, 0:QW], NSL, IOTAR, start=True, stop=True),
                reads=["pr"], writes=["psA"])
            P.op("v", lambda e: e.tensor_copy(out=P1Q[0:H, :],
                                              in_=psA[0:H, 0:QW]),
                 reads=["psA", "pr"], writes=["pr"])
            P.op("v", lambda e: e.tensor_copy(out=RQ[0:H, :],
                                              in_=P1Q[0:H, :]),
                 reads=["pr"], writes=["pr"])
            P.op("v", lambda e: e.tensor_tensor(RQ[0:H, :], psA[0:H, 0:QW],
                                                RQ[0:H, :], ALU.subtract),
                 reads=["psA", "pr"], writes=["pr"])
            P.op("v", lambda e: e.tensor_copy(out=P2Q[0:H, :],
                                              in_=RQ[0:H, :]),
                 reads=["pr"], writes=["pr"])
            P.op("v", lambda e: e.tensor_copy(out=RQ2[0:H, :],
                                              in_=P2Q[0:H, :]),
                 reads=["pr"], writes=["pr"])
            P.op("v", lambda e: e.tensor_tensor(RQ2[0:H, :], RQ[0:H, :],
                                                RQ2[0:H, :], ALU.subtract),
                 reads=["pr"], writes=["pr"])
            P.op("v", lambda e: e.tensor_copy(out=P3Q[0:H, :],
                                              in_=RQ2[0:H, :]),
                 reads=["pr"], writes=["pr"])
            for part, t in ((0, P1Q), (1, P2Q), (2, P3Q)):
                P.op("g", lambda e, part=part, t=t, qq=qq: e.dma_start(
                    out=bass.AP(p123d, part * H * n_tok + qq * QW,
                                [[n_tok, H], [1, QW]]),
                    in_=t[0:H, :]),
                    reads=["pr"], writes=["p123d"], kind="dma")

        from collections import deque as _dq
        p123_fill = _dq(collect(emit_p123))

        NPASS = (QB + 1) // 2
        def emit_kq0_pass(ps):
            qbs = [qb for qb in (2 * ps, 2 * ps + 1) if qb < QB]
            for kc in range(KC):
                for j, qb in enumerate(qbs):
                    P.op("t", lambda e, kc=kc, qb=qb, j=j:
                         e.matmul(psS[j][:, 0:QBW],
                                  _kqw0[:, kc * 128:(kc + 1) * 128],
                                  XNT[kc][:, qb * QBW:(qb + 1) * QBW],
                                  start=(kc == 0), stop=(kc == KC - 1)),
                         reads=["kqw0"] +
                               [f"xnt{kc}:{t}" for t in
                                range(qb * QBW // 128, (qb + 1) * QBW // 128)],
                         writes=[f"psS{j}"])
            for j, qb in enumerate(qbs):
                P.op("s", lambda e, j=j, qb=qb:
                     e.activation(KTA[0][0:64, qb * QBW:(qb + 1) * QBW],
                                  psS[j][0:64, 0:QBW], AF.Identity,
                                  bias=BKQ[0:64, 0:1], scale=1.0),
                     reads=[f"psS{j}", "bkq", "kta0"],
                     writes=["kta0"])
                P.op("s", lambda e, j=j, qb=qb:
                     e.activation(QTA[0][0:64, qb * QBW:(qb + 1) * QBW],
                                  psS[j][64:128, 0:QBW], AF.Identity,
                                  bias=BKQ[64:128, 0:1], scale=0.125),
                     reads=[f"psS{j}", "bkq", "qta0"],
                     writes=["qta0"])
        def emit_kq0_aug():
            P.op("g", lambda e: e.dma_start(
                out=QTA[0][64:67, :],
                in_=bass.AP(p123d, 0, [[H * n_tok, 3], [1, n_tok]])),
                reads=["p123d", "qta0"], writes=["qta0"], kind="dma")

        GRP = min(4, TT)
        kq0_done = 0
        for g0 in range(0, TT, GRP):
            gtiles = list(range(g0, min(TT, g0 + GRP)))
            for tt in gtiles:
                a = f"acc{tt}"
                sm = SM[tt % 4]
                smk = f"sm{tt % 4}"
                sq = STG[0] if tt % 2 == 0 else SQB[0]
                sqk = "stg0" if tt % 2 == 0 else "sqb1"
                P.op("s", lambda e, tt=tt, sq=sq, sm=sm: e.activation(
                    sq[:, 0:d], ACC[tt][:], AF.Square,
                    accum_out=sm[:, 1:2]),
                    reads=[a, smk], writes=[sqk, smk])
                P.op("v", lambda e, tt=tt, sm=sm: e.tensor_reduce(
                    sm[:, 0:1], ACC[tt][:], AX.X, ALU.add),
                    reads=[a, smk], writes=[smk])
                P.op("v", lambda e, sm=sm: e.tensor_scalar(
                    sm[:, 2:3], sm[:, 0:1], 1.0 / d, None, ALU.mult),
                    reads=[smk], writes=[smk])
                P.op("v", lambda e, sm=sm: e.tensor_tensor(
                    sm[:, 3:4], sm[:, 2:3], sm[:, 2:3], ALU.mult),
                    reads=[smk], writes=[smk])
                P.op("v", lambda e, sm=sm: e.scalar_tensor_tensor(
                    sm[:, 4:5], sm[:, 1:2], 1.0 / d, sm[:, 3:4], ALU.mult,
                    ALU.subtract),
                    reads=[smk], writes=[smk])
            for tt in gtiles:
                sm = SM[tt % 4]
                smk = f"sm{tt % 4}"
                P.op("s", lambda e, sm=sm: e.activation(
                    sm[:, 5:6], sm[:, 4:5], AF.Sqrt, bias=EPS[:, 0:1], scale=1.0),
                    reads=[smk, "eps"], writes=[smk])
            for tt in gtiles:
                a = f"acc{tt}"
                sm = SM[tt % 4]
                smk = f"sm{tt % 4}"
                P.op("v", lambda e, sm=sm: e.reciprocal(sm[:, 6:7], sm[:, 5:6]),
                     reads=[smk], writes=[smk])
                P.op("v", lambda e, sm=sm: e.scalar_tensor_tensor(
                    sm[:, 7:8], sm[:, 2:3], -1.0, sm[:, 6:7], ALU.mult, ALU.mult),
                    reads=[smk], writes=[smk])
                xn = XNTOK[tt % 4]
                xnk = f"xntok{tt % 4}"
                P.op("s", lambda e, tt=tt, sm=sm, xn=xn: e.activation(
                    xn[:], ACC[tt][:], AF.Identity, bias=sm[:, 7:8],
                    scale=sm[:, 6:7]),
                    reads=[a, smk], writes=[xnk])
                P.op("q", lambda e, tt=tt, xn=xn: e.dma_start(
                    out=xnd.ap()[tt * 128:(tt + 1) * 128, :], in_=xn[:]),
                    reads=[xnk], writes=["xnd"], kind="dma")
                psT, ptk = (psA, "psA") if tt % 2 == 0 else (psD, "psD")
                pab = psT.ap().bitcast(BF16)
                for j in range(KC):
                    P.op("t", lambda e, j=j, xn=xn, pab=pab: e.transpose(
                        pab[:, j * 128:(j + 1) * 128], xn[:, j * 128:(j + 1) * 128],
                        IDB[:]), reads=[xnk, "idb"], writes=[ptk])
                for j in range(KC):
                    P.op("v", lambda e, j=j, tt=tt, pab=pab: e.tensor_copy(
                        out=XNT[j][:, tt * 128:(tt + 1) * 128],
                        in_=pab[:, j * 128:(j + 1) * 128]),
                        reads=[ptk], writes=[f"xnt{j}:{tt}"])
                for kc in range(KC):
                    P.op("t", lambda e, kc=kc, tt=tt: e.matmul(
                        psB[:, 0:E], XNT[kc][:, tt * 128:(tt + 1) * 128],
                        WGB[:, kc * E:(kc + 1) * E], start=(kc == 0),
                        stop=(kc == KC - 1)),
                        reads=[f"xnt{kc}:{tt}", "wgb"], writes=["psB"])
                P.op("v", lambda e, tt=tt: e.tensor_copy(
                    out=LGT[:, tt * E:(tt + 1) * E], in_=psB[:, 0:E]),
                    reads=["psB"], writes=[f"lgt{tt}"])
                srt = SRT[tt % 4]
                sk = f"srt{tt % 4}"
                P.op("v", lambda e, srt=srt, tt=tt: e.tensor_reduce(
                    srt[:, 0:1], LGT[:, tt * E:(tt + 1) * E], AX.X, ALU.max),
                    reads=[f"lgt{tt}"], writes=[sk])
                P.op("v", lambda e, srt=srt: e.tensor_scalar(
                    srt[:, 1:2], srt[:, 0:1], -1.0, None, ALU.mult),
                    reads=[sk], writes=[sk])
                for _ in range(3):
                    if not p123_fill:
                        break
                    p123_fill.popleft()()
            for tt in gtiles:
                srt = SRT[tt % 4]
                sk = f"srt{tt % 4}"
                P.op("s", lambda e, srt=srt, tt=tt: e.activation(
                    PROBS[tt][:], LGT[:, tt * E:(tt + 1) * E], AF.Exp,
                    bias=srt[:, 1:2], scale=1.0),
                    reads=[f"lgt{tt}", sk], writes=[f"probs{tt}"])
            for tt in gtiles:
                srt = SRT[tt % 4]
                sk = f"srt{tt % 4}"
                P.op("v", lambda e, srt=srt, tt=tt: e.tensor_reduce(
                    srt[:, 2:3], PROBS[tt][:], AX.X, ALU.add),
                    reads=[f"probs{tt}", sk], writes=[sk])
                P.op("v", lambda e, srt=srt: e.reciprocal(srt[:, 3:4], srt[:, 2:3]),
                     reads=[sk], writes=[sk])
                P.op("v", lambda e, srt=srt, tt=tt: e.tensor_scalar(
                    PROBS[tt][:], PROBS[tt][:], srt[:, 3:4], None, ALU.mult),
                    reads=[sk, f"probs{tt}"], writes=[f"probs{tt}"])
                g4 = min(4, TT)
                off = 128 * (tt % g4)
                P.op("t", lambda e, tt=tt, off=off: e.transpose(
                    bass.AP(psC, off, [[512, E], [1, 128]]), PROBS[tt][:], IDF[:]),
                    reads=[f"probs{tt}", "idf"], writes=["psC"])
                if tt % g4 == g4 - 1:
                    q = tt // g4
                    w4 = g4 * 128
                    P.op("v", lambda e, q=q, w4=w4: e.tensor_copy(
                        out=PRT[:, q * w4:(q + 1) * w4],
                        in_=bass.AP(psC, 0, [[512, E], [1, w4]])),
                        reads=["psC"], writes=["prt"])
            while (kq0_done < NPASS and
                   min(TT, g0 + GRP) * 128 >= 2 * (kq0_done + 1) * QBW):
                emit_kq0_pass(kq0_done)
                kq0_done += 1
        while p123_fill:
            p123_fill.popleft()()
        while kq0_done < NPASS:
            emit_kq0_pass(kq0_done)
            kq0_done += 1
        emit_kq0_aug()
        P.op("q", lambda e: e.dma_start(out=cc_in.ap(), in_=PRT[:]),
             reads=["prt"], writes=["cc_in"], kind="dma")

        def emit_collective():
            P.op("g", lambda e: e.collective_compute(
                "AllGather", ALU.bypass, replica_groups=[list(range(NC))],
                ins=[cc_in.ap()], outs=[cc_out.ap()]),
                reads=["cc_in"], writes=["cc_out"], kind="cc")
        P.op("v", lambda e: e.memset(PR[:], 0.0), writes=["pr"])
        P.op("v", lambda e: e.memset(TCNT[:], 0.0), writes=["tcnt"])
        P.op("v", lambda e: e.memset(LO[:], 0.0), writes=["lo"])
        P.op("v", lambda e: e.memset(HI[:], 1.0000001), writes=["hi"])
        P.op("v", lambda e: e.memset(CMPB[:], 0.0), writes=["cmpb"])
        def emit_prd():
            for r in range(NC):
                for hf in range(HFC):
                    P.op("q", lambda e, r=r, hf=hf: e.dma_start(
                        out=bass.AP(prd, (r * HFC + hf) * PRW,
                                    [[PPE * PRW, E], [1, U]]),
                        in_=bass.AP(cc_out, (r * E) * n_tok + hf * U,
                                    [[n_tok, E], [1, U]])),
                        reads=["cc_out"], writes=["prdram"], kind="dma")

        def emit_zero_xgd():
            for blk in range(E * PAD // 128):
                P.op("q", lambda e, blk=blk: e.dma_start(
                    out=xgd.ap()[blk * 128:(blk + 1) * 128, :],
                    in_=c_zeros.ap()),
                    writes=["xgd"], kind="dma")

        def emit_pr_load():
            P.op("q", lambda e: e.dma_start(out=PR[:], in_=prd.ap()),
                 reads=["prdram", "pr", "tcnt", "lo", "hi", "cmpb"],
                 writes=["pr"], kind="dma")

        def emit_search_iters(a, b):
            for _ in range(a, b):
                P.op("v", lambda e: e.tensor_tensor(MID[:], LO[:], HI[:], ALU.add),
                     reads=["lo", "hi"], writes=["mid"])
                P.op("v", lambda e: e.tensor_scalar(MID[:], MID[:], 0.5, None,
                                                    ALU.mult),
                     reads=["mid"], writes=["mid"])
                P.op("v", lambda e: e.tensor_scalar(
                    CMPB[:], PR[:], MID[:, 0:1], None, ALU.is_ge),
                    reads=["pr", "mid"], writes=["cmpb"])
                P.op("v", lambda e: e.tensor_reduce(
                    TCNT[:, 0:1], CMPB[:], AX.X, ALU.add),
                    reads=["cmpb", "tcnt"], writes=["tcnt"])
                P.op("v", lambda e: e.transpose(TCT[:], TCNT[:]),
                     reads=["tcnt"], writes=["tct"])
                P.op("v", lambda e: e.tensor_reduce(
                    CNT2[:], TCT.ap().rearrange("p (g u) -> p g u", u=PPE),
                    AX.X, ALU.add), reads=["tct"], writes=["cnt2"])
                P.op("v", lambda e: e.tensor_scalar(
                    GEB[:], CNT2[:], float(cap), None, ALU.is_ge),
                    reads=["cnt2"], writes=["geb"])
                P.op("v", lambda e: e.tensor_copy(
                    out=TG[:, 0:PPE], in_=GEB[:, 0:1].to_broadcast((128, PPE))),
                    reads=["geb", "tg"], writes=["tg"])
                P.op("v", lambda e: e.tensor_copy(
                    out=TG[:, PPE:2 * PPE],
                    in_=GEB[:, 1:2].to_broadcast((128, PPE))),
                    reads=["geb", "tg"], writes=["tg"])
                P.op("v", lambda e: e.transpose(TGT[:], TG[:]),
                     reads=["tg"], writes=["tgt"])
                P.op("v", lambda e: e.tensor_tensor(DD1[:], MID[:], LO[:],
                                                    ALU.subtract),
                     reads=["mid", "lo"], writes=["dd1"])
                P.op("v", lambda e: e.scalar_tensor_tensor(
                    LO[:], DD1[:], TGT[:, 0:1], LO[:], ALU.mult, ALU.add),
                    reads=["dd1", "tgt", "lo"], writes=["lo"])
                P.op("v", lambda e: e.tensor_tensor(DD1[:], HI[:], MID[:],
                                                    ALU.subtract),
                     reads=["hi", "mid"], writes=["dd1"])
                P.op("v", lambda e: e.scalar_tensor_tensor(
                    HI[:], DD1[:], TGT[:, 0:1], MID[:], ALU.mult, ALU.add),
                    reads=["dd1", "tgt", "mid"], writes=["hi"])

        def emit_gates_part(t0, t1):
            if t0 == 0:
                P.op("g", lambda e: e.dma_start(
                    out=bass.AP(thd, 0, [[1, 128], [1, 1]]), in_=LO[:]),
                    reads=["lo"], writes=["thd"], kind="dma")
                P.op("g", lambda e: e.dma_start(
                    out=THR8[:], in_=bass.AP(thd, 0, [[1, 1], [PPE, E]]),
                    allow_slow_non_contiguous=True),
                    reads=["thd"], writes=["thr8"], kind="dma")
                P.op("t", lambda e: e.matmul(psA[:, 0:E], ONES1F[0:1, :], THR8[:],
                                             start=True, stop=True),
                     reads=["thr8", "ones1f"], writes=["psA"])
                P.op("v", lambda e: e.tensor_copy(out=THRB[:], in_=psA[:, 0:E]),
                     reads=["psA"], writes=["thrb"])
            for tt in range(t0, t1):
                P.op("v", lambda e, tt=tt: e.tensor_tensor(
                    GSC[:, 0:E], PROBS[tt][:], THRB[:], ALU.is_ge),
                    reads=[f"probs{tt}", "thrb", "gsc"], writes=["gsc"])
                P.op("v", lambda e, tt=tt: e.scalar_tensor_tensor(
                    GSC[:, E:2 * E], GSC[:, 0:E], 2.0, PROBS[tt][:],
                    ALU.mult, ALU.add),
                    reads=["gsc", f"probs{tt}"], writes=["gsc"])
                P.op("v", lambda e: e.tensor_reduce(
                    GSC[:, 2 * E:2 * E + 1], GSC[:, E:2 * E], AX.X, ALU.max),
                    reads=["gsc"], writes=["gsc"])
                P.op("v", lambda e: e.tensor_scalar(
                    GSC[:, 3 * E:4 * E], GSC[:, E:2 * E], GSC[:, 2 * E:2 * E + 1],
                    None, ALU.is_ge), reads=["gsc"], writes=["gsc"])
                P.op("v", lambda e: e.tensor_tensor(
                    GSC[:, 4 * E:5 * E], GSC[:, 3 * E:4 * E], EIO99[:], ALU.mult),
                    reads=["gsc", "eio99"], writes=["gsc"])
                P.op("v", lambda e: e.tensor_scalar(
                    GSC[:, 4 * E:5 * E], GSC[:, 4 * E:5 * E], 99.0, None, ALU.add),
                    reads=["gsc"], writes=["gsc"])
                P.op("v", lambda e: e.tensor_reduce(
                    GSC[:, 5 * E:5 * E + 1], GSC[:, 4 * E:5 * E], AX.X, ALU.min),
                    reads=["gsc"], writes=["gsc"])
                P.op("v", lambda e, tt=tt: e.tensor_scalar(
                    OH[:, tt * E:(tt + 1) * E], EIO[:], GSC[:, 5 * E:5 * E + 1],
                    None, ALU.is_equal), reads=["gsc", "eio", "oh"], writes=["oh"])
                P.op("v", lambda e, tt=tt: e.tensor_copy(
                    out=TOKE[:, tt:tt + 1], in_=GSC[:, 5 * E:5 * E + 1]),
                    reads=["gsc", "toke"], writes=["toke"])
                P.op("v", lambda e, tt=tt: e.tensor_tensor(
                    GSC[:, 7 * E:8 * E], OH[:, tt * E:(tt + 1) * E], PROBS[tt][:],
                    ALU.mult), reads=["gsc", "oh", f"probs{tt}"], writes=["gsc"])
                P.op("v", lambda e: e.tensor_reduce(
                    GSC[:, 8 * E:8 * E + 1], GSC[:, 7 * E:8 * E], AX.X, ALU.add),
                    reads=["gsc"], writes=["gsc"])
                P.op("v", lambda e, tt=tt: e.tensor_copy(
                    out=WVS[:, tt:tt + 1], in_=GSC[:, 8 * E:8 * E + 1]),
                    reads=["gsc"], writes=["wvs"])

        early.close()
        with ExitStack() as att:
            asb = lambda n, s, t: att.enter_context(nc.sbuf_tensor(n, s, t))
            apt = lambda n, s: att.enter_context(nc.psum_tensor(n, s, F32))
            YT = [asb(f"yt{k}", [128, n_tok], BF16) for k in range(KC)]
            RECQ = asb("recq", [64, QBW], F32)
            RECR = [asb(f"recr{q}", [1, QBW], F32).ap()[0:1, :] for q in range(QB)]
            psY = apt("psY", [65, 2 * QBW])
            from contextlib import ExitStack as _ES4
            heads_scope = _ES4()
            hsb = lambda n, sh, dt: heads_scope.enter_context(nc.sbuf_tensor(n, sh, dt))
            KQW = [_kqw0, _kqw0]
            V4 = [hsb(f"v4_{i}", [128, TT * HPG * (dh + 1)], BF16) for i in range(2)]
            WVB = [_wvb0, _wvb0]
            PTB = [hsb(f"ptb{i}", [128, QBW], BF16) for i in range(3)]

            for i in range(2):
                P.op("v", lambda e, i=i: e.memset(KTA[i][64:67, :], 1.0),
                     writes=[f"ktaones{i}"])
                P.op("v", lambda e, i=i: e.memset(
                    bass.AP(V4[i], dh, [[TT * HPG * (dh + 1), 128],
                                        [dh + 1, TT * HPG], [1, 1]]), 1.0),
                    writes=[f"v4ones{i}"])

            def stage_wv(g, buf):
                P.op("g", lambda e, g=g: e.dma_start(
                    out=WVB[0].ap().rearrange("p (k f) -> p k f", f=SW),
                    in_=bass.AP(wkqv, 2 * d + g * SW,
                                [[3 * d, 128], [3 * d * 128, KC], [1, SW]])),
                    reads=["wvb0"], writes=["wvb0"], kind="dma")

            def produce_v(g, buf):
                for tt in range(TT):
                    psh, phk = (psA, "psA") if tt % 2 == 0 else (psB, "psB")
                    for kc in range(KC):
                        P.op("t", lambda e, kc=kc, tt=tt, buf=buf, psh=psh: e.matmul(
                            psh[:, 0:SW], XNT[kc][:, tt * 128:(tt + 1) * 128],
                            WVB[buf][:, kc * SW:(kc + 1) * SW],
                            start=(kc == 0), stop=False),
                            reads=[f"xnt{kc}:{tt}", "wvb0"], writes=[phk])
                    P.op("t", lambda e, tt=tt, g=g, psh=psh: e.matmul(
                        psh[:, 0:SW], ONESB[0:1, tt * 128:(tt + 1) * 128],
                        BVHI[0:1, g * SW:(g + 1) * SW], start=False, stop=False),
                        reads=["ones2b", "bvhi"], writes=[phk])
                    P.op("t", lambda e, tt=tt, g=g, psh=psh: e.matmul(
                        psh[:, 0:SW], ONESB[0:1, tt * 128:(tt + 1) * 128],
                        BVLO[0:1, g * SW:(g + 1) * SW], start=False, stop=True),
                        reads=["ones2b", "bvlo"], writes=[phk])
                    dst = bass.AP(V4[buf], tt * HPG * (dh + 1),
                                  [[TT * HPG * (dh + 1), 128], [dh + 1, HPG], [1, dh]])
                    src = psh.ap()[:, 0:SW].rearrange("p (h f) -> p h f", f=dh)
                    P.op("v", lambda e, dst=dst, src=src: e.tensor_copy(
                        out=dst, in_=src),
                        reads=[phk, f"v4_{buf}:{tt}"], writes=[f"v4_{buf}:{tt}"])

            def stage_kq(h, buf):
                for col, base in ((0, h * dh), (1, d + h * dh)):
                    P.op("g", lambda e, base=base, col=col: e.dma_start(
                        out=bass.AP(KQW[0], col * dh,
                                    [[KC * 128, 128], [128, KC], [1, dh]]),
                        in_=bass.AP(wkqv, base, [[3 * d, 128], [3 * d * 128, KC],
                                                 [1, dh]])),
                        reads=["kqw0"], writes=["kqw0", "kqw1"], kind="dma")

            def produce_kq(h, buf):
                npass = (QB + 1) // 2
                for ps in range(npass):
                    qbs = [qb for qb in (2 * ps, 2 * ps + 1) if qb < QB]
                    for kc in range(KC):
                        for j, qb in enumerate(qbs):
                            P.op("t", lambda e, kc=kc, qb=qb, j=j, buf=buf:
                                 e.matmul(psS[j][:, 0:QBW],
                                          KQW[buf][:, kc * 128:(kc + 1) * 128],
                                          XNT[kc][:, qb * QBW:(qb + 1) * QBW],
                                          start=(kc == 0), stop=(kc == KC - 1)),
                                 reads=["kqw0"] +
                                       [f"xnt{kc}:{t}" for t in
                                        range(qb * QBW // 128, (qb + 1) * QBW // 128)],
                                 writes=[f"psS{j}"])
                    for j, qb in enumerate(qbs):
                        P.op("s", lambda e, j=j, qb=qb, buf=buf, h=h:
                             e.activation(KTA[buf][0:64, qb * QBW:(qb + 1) * QBW],
                                          psS[j][0:64, 0:QBW], AF.Identity,
                                          bias=BKQ[0:64, h:h + 1], scale=1.0),
                             reads=[f"psS{j}", "bkq", f"kta{buf}"],
                             writes=[f"kta{buf}"])
                        P.op("s", lambda e, j=j, qb=qb, buf=buf, h=h:
                             e.activation(QTA[buf][0:64, qb * QBW:(qb + 1) * QBW],
                                          psS[j][64:128, 0:QBW], AF.Identity,
                                          bias=BKQ[64:128, h:h + 1], scale=0.125),
                             reads=[f"psS{j}", "bkq", f"qta{buf}"],
                             writes=[f"qta{buf}"])
                P.op("g", lambda e, h=h, buf=buf: e.dma_start(
                    out=QTA[buf][64:67, :],
                    in_=bass.AP(p123d, h * n_tok, [[H * n_tok, 3], [1, n_tok]])),
                    reads=["p123d", f"qta{buf}"], writes=[f"qta{buf}"], kind="dma")

            from collections import deque as _dq5
            pending_norms = _dq5()

            def attend(h, buf, vbuf, fill=None):
                gih = h % HPG
                ytk = f"yt{(h * dh) // 128}"
                yro = (h * dh) % 128
                ring = [(psS[0], "psS0"), (psS[1], "psS1"), (psC, "psC"),
                        (psD, "psD")]
                NPTB = len(PTB)
                kta_r = [f"kta{buf}", f"ktaones{buf}"]
                qta_r = [f"qta{buf}"]

                def norm_recip(qb):
                    off = (qb % 2) * QBW
                    P.op("v", lambda e, qb=qb, off=off: e.reciprocal(
                        RECR[qb], psY[64:65, off:off + QBW]),
                        reads=[f"psY{qb % 2}", f"recr{qb}", "xnd"],
                        writes=[f"recr{qb}"])

                def norm_rest(qb, ytk=None, yro=None, h=h):
                    ytk = ytk if ytk is not None else f"yt{(h * dh) // 128}"
                    yro = yro if yro is not None else (h * dh) % 128
                    off = (qb % 2) * QBW
                    P.op("t", lambda e, qb=qb: e.matmul(
                        psB[0:64, 0:QBW], ONES1F[0:1, 0:64], RECR[qb],
                        start=True, stop=True),
                        reads=[f"recr{qb}", "ones1f"], writes=["psB"])
                    P.op("s", lambda e: e.copy(RECQ[0:64, :], psB[0:64, 0:QBW]),
                        reads=["psB", "recq", "xnd"], writes=["recq"])
                    P.op("v", lambda e, qb=qb, off=off, yro=yro, h=h:
                         e.tensor_tensor(
                             YT[(h * dh) // 128][yro:yro + 64,
                                                 qb * QBW:(qb + 1) * QBW],
                             psY[0:64, off:off + QBW],
                             RECQ[0:64, :], ALU.mult),
                        reads=[f"psY{qb % 2}", "recq", ytk, "xnd", "cc_in"],
                        writes=[ytk])

                chunks = []
                for qb in range(QB):
                    lk = ((qb + 1) * QBW - 1) // 128
                    for ki in range(lk + 1):
                        q0 = max(qb * QBW, ki * 128)
                        chunks.append((qb, ki, q0, (qb + 1) * QBW - q0,
                                       ki == lk))
                LAG = 2
                norm_later = []

                def emit_scores(i):
                    qb, ki, q0, wq, is_last = chunks[i]
                    pse, pk = ring[i % 4]
                    P.op("t", lambda e, ki=ki, q0=q0, wq=wq, pse=pse, buf=buf:
                         e.matmul(pse[:, 0:wq],
                                  KTA[buf][0:67, ki * 128:(ki + 1) * 128],
                                  QTA[buf][0:67, q0:q0 + wq],
                                  start=True, stop=True),
                         reads=kta_r + qta_r, writes=[pk])
                    if q0 == ki * 128:
                        P.op("v", lambda e, pse=pse: e.tensor_tensor(
                            pse[:, 0:128], pse[:, 0:128], CM[:], ALU.add),
                            reads=["cm", pk], writes=[pk])
                    pb = PTB[i % NPTB]
                    pbk = f"ptb{i % NPTB}"
                    P.op("s", lambda e, pse=pse, pb=pb, ki=ki, wq=wq, h=h:
                         e.activation(pb[:, 0:wq], pse[:, 0:wq], AF.Exp,
                                      bias=BJ[:, h * KI + ki:h * KI + ki + 1],
                                      scale=1.0),
                         reads=[pk, f"bj{h}", pbk], writes=[pbk])

                def emit_av(i):
                    qb, ki, q0, wq, is_last = chunks[i]
                    pb = PTB[i % NPTB]
                    pbk = f"ptb{i % NPTB}"
                    off = (qb % 2) * QBW
                    c0 = off + (q0 - qb * QBW)
                    vsl = bass.AP(V4[vbuf],
                                  ki * HPG * (dh + 1) + gih * (dh + 1),
                                  [[TT * HPG * (dh + 1), 128], [1, dh + 1]])
                    P.op("t", lambda e, vsl=vsl, pb=pb, c0=c0, wq=wq, ki=ki,
                         lk=is_last: e.matmul(
                             psY[0:65, c0:c0 + wq], vsl, pb[:, 0:wq],
                             start=(ki == 0), stop=lk),
                         reads=[f"v4_{vbuf}:{ki}", f"v4ones{vbuf}", pbk],
                         writes=[f"psY{qb % 2}"])
                    if is_last:
                        norm_recip(qb)
                        norm_later.append((qb, i))

                N = len(chunks)
                for i in range(N + LAG):
                    if i < N:
                        emit_scores(i)
                    if i - LAG >= 0:
                        emit_av(i - LAG)
                    while norm_later and norm_later[0][1] <= i - LAG - 8:
                        norm_rest(norm_later.pop(0)[0])
                    if fill:
                        for _ in range(2):
                            if not fill:
                                break
                            fill.popleft()()
                ytk_h, yro_h = ytk, yro
                for qb, _ in norm_later:
                    pending_norms.append(
                        lambda qb=qb: norm_rest(qb, ytk_h, yro_h, h))
            def emit_slots_part(t0, t1):
              if t0 == 0:
                P.op("v", lambda e: e.tensor_reduce(
                  RUNF[:],
                  bass.AP(OH, 0, [[TT * E, 128], [1, E], [E, TT]]),
                  AX.X, ALU.add), reads=["oh"], writes=["runf"])
                P.op("v", lambda e: e.tensor_copy(out=PCB[:], in_=RUNF[:]),
                     reads=["runf"], writes=["pcb"])
                P.op("t", lambda e: e.matmul(psB[:, 0:E], TRILS[:], PCB[:],
                                             start=True, stop=True),
                     reads=["trils", "pcb"], writes=["psB"])
                P.op("v", lambda e: e.tensor_copy(out=PPRE[:], in_=psB[:, 0:E]),
                     reads=["psB"], writes=["ppre"])
                P.op("v", lambda e: e.memset(RUNF[:], 0.0),
                     reads=["runf"], writes=["runf"])
                P.op("v", lambda e: e.memset(C1E9[:], 1.0e9), writes=["c1e9"])
              for tt in range(t0, t1):
                  P.op("v", lambda e: e.tensor_tensor(
                      GSC[:, 0:E], PPRE[:], RUNF[:], ALU.add),
                      reads=["ppre", "runf", "gsc"], writes=["gsc"])
                  P.op("v", lambda e, tt=tt: e.tensor_tensor(
                      GSC[:, E:2 * E], GSC[:, 0:E], OH[:, tt * E:(tt + 1) * E],
                      ALU.mult), reads=["gsc", "oh"], writes=["gsc"])
                  P.op("v", lambda e: e.tensor_reduce(
                      GSC[:, 2 * E:2 * E + 1], GSC[:, E:2 * E], AX.X, ALU.add),
                      reads=["gsc"], writes=["gsc"])
                  P.op("v", lambda e, tt=tt: e.scalar_tensor_tensor(
                      GSC[:, 3 * E:3 * E + 1], TOKE[:, tt:tt + 1], float(PAD),
                      GSC[:, 2 * E:2 * E + 1], ALU.mult, ALU.add),
                      reads=["gsc", "toke"], writes=["gsc"])
                  P.op("v", lambda e: e.tensor_scalar(
                      GSC[:, 4 * E:4 * E + 1], GSC[:, 2 * E:2 * E + 1], float(PAD),
                      None, ALU.is_lt), reads=["gsc"], writes=["gsc"])
                  P.op("v", lambda e: e.tensor_scalar(
                      GSC[:, 5 * E:5 * E + 1], GSC[:, 4 * E:4 * E + 1], -1.0e6,
                      1.0e6, ALU.mult, ALU.add), reads=["gsc"], writes=["gsc"])
                  P.op("v", lambda e: e.scalar_tensor_tensor(
                      GSC[:, 6 * E:6 * E + 1], GSC[:, 3 * E:3 * E + 1],
                      GSC[:, 4 * E:4 * E + 1], GSC[:, 5 * E:5 * E + 1],
                      ALU.mult, ALU.add),
                      reads=["gsc"], writes=["gsc"])
                  P.op("v", lambda e, tt=tt: e.tensor_copy(
                      out=SLOTI[:, tt:tt + 1], in_=GSC[:, 6 * E:6 * E + 1]),
                      reads=["gsc"], writes=["sloti"])
                  P.op("v", lambda e, tt=tt: e.tensor_tensor(
                      RUNF[:], RUNF[:], OH[:, tt * E:(tt + 1) * E], ALU.add),
                      reads=["runf", "oh"], writes=["runf"])
            def emit_scatter():
                if PRW >= d:
                    bounces = [(CMPB.ap()[:, 0:d], ["cmpb"]),
                               (PR.ap().bitcast(BF16)[:, 0:d], ["pr"])]
                else:
                    bounces = [(KQW[0].ap()[:, 0:d], ["kqw0", "kqw1"])]
                    if TT * HPG * (dh + 1) >= d:
                        bounces.append(
                            (bass.AP(V4[0], 0,
                                     [[TT * HPG * (dh + 1), 128], [1, d]]),
                             [f"v4_0:{i}" for i in range(TT)] + ["v4ones0"]))
                for tt in range(TT):
                    xt, xtks = bounces[tt % len(bounces)]
                    P.op("q", lambda e, tt=tt, xt=xt: e.dma_start(
                        out=xt, in_=xnd.ap()[tt * 128:(tt + 1) * 128, :]),
                        reads=["xnd"] + xtks, writes=xtks, kind="dma")
                    P.op("g", lambda e, tt=tt, xt=xt: e.indirect_dma_start(
                        out=xgd.ap(),
                        out_offset=IndirectOffsetOnAxis(SLOTI[:, tt:tt + 1], 0),
                        in_=xt,
                        in_offset=None,
                        bounds_check=E * PAD - 1,
                        oob_is_err=False),
                        reads=["sloti"] + xtks, writes=["xgd"] + xtks, kind="dma")
            produce_v(0, 0)

            from collections import deque
            emitted_iters = 0
            scatter_done = False
            SI = cfg["search_iters"]
            fill = deque()
            for h in range(H):
                g = h // HPG
                buf = h % 2
                vbuf = g % 2
                if h % HPG == 0 and g + 1 < NG:
                    stage_wv(g + 1, (g + 1) % 2)
                if h + 1 < H:
                    stage_kq(h + 1, (h + 1) % 2)
                    produce_kq(h + 1, (h + 1) % 2)
                while pending_norms:
                    pending_norms.popleft()()
                if H >= 14:
                    if 1 <= h <= 8:
                        a0, a1 = (h - 1) * SI // 8, h * SI // 8
                        fill.extend(collect(
                            lambda: emit_search_iters(a0, a1)))
                    elif h == 9:
                        fill.extend(collect(
                            lambda: emit_gates_part(0, TT // 2)))
                    elif h == 10:
                        fill.extend(collect(
                            lambda: emit_gates_part(TT // 2, TT)))
                    elif h == 11:
                        fill.extend(collect(
                            lambda: emit_slots_part(0, TT // 2)))
                    elif h == 12:
                        fill.extend(collect(
                            lambda: emit_slots_part(TT // 2, TT)))
                attend(h, buf, vbuf, fill)
                if h == 0:
                    emit_collective()
                    emit_prd()
                    emit_pr_load()
                    emit_zero_xgd()
                if h % HPG == 0 and g + 1 < NG:
                    produce_v(g + 1, (g + 1) % 2)
                if H >= 14:
                    if h == H - 2 and PRW >= d:
                        emit_scatter()
                        scatter_done = True
                else:
                    if h >= 1 and emitted_iters < SI:
                        step = max(1, SI // max(1, (H - 2)))
                        nxt = min(SI, emitted_iters + step)
                        emit_search_iters(emitted_iters, nxt)
                        emitted_iters = nxt
            while pending_norms:
                pending_norms.popleft()()
            while fill:
                fill.popleft()()
            if H < 14:
                if emitted_iters < SI:
                    emit_search_iters(emitted_iters, SI)
                emit_gates_part(0, TT)
                emit_slots_part(0, TT)
            if not scatter_done:
                emit_scatter()
            heads_scope.close()
            wpb_extra = ["xgd"] if PRW < d else []

            with ExitStack() as prj:
                WPB = [prj.enter_context(nc.sbuf_tensor(f"wpb{k}", [128, d], BF16))
                       for k in range(KC)]
                ytall = [f"yt{k}" for k in range(KC)]
                for kc in range(KC):
                    P.op("g", lambda e, kc=kc: e.dma_start(
                        out=WPB[kc][:],
                        in_=wproj.ap()[kc * 128:(kc + 1) * 128, :]),
                        reads=["p123d"] + ytall + wpb_extra,
                        writes=[f"wpb{kc}"], kind="dma")
                ytkeys = [f"yt{k}" for k in range(KC)]
                for tt in range(TT):
                    for db in range(DBLK):
                        psp, ppk = ((psA, "psA") if (tt * DBLK + db) % 2 == 0
                                    else (psB, "psB"))
                        for kc in range(KC):
                            P.op("t", lambda e, kc=kc, tt=tt, db=db, psp=psp:
                                 e.matmul(psp[:, 0:DW],
                                          YT[kc][:, tt * 128:(tt + 1) * 128],
                                          WPB[kc][:, db * DW:(db + 1) * DW],
                                          start=(kc == 0), stop=False),
                                 reads=ytkeys + [f"wpb{kc}"], writes=[ppk])
                        P.op("t", lambda e, tt=tt, db=db, psp=psp: e.matmul(
                            psp[:, 0:DW], ONESB[0:1, tt * 128:(tt + 1) * 128],
                            BPHI[0:1, db * DW:(db + 1) * DW],
                            start=False, stop=False),
                            reads=["ones2b", "bphi"], writes=[ppk])
                        P.op("t", lambda e, tt=tt, db=db, psp=psp: e.matmul(
                            psp[:, 0:DW], ONESB[0:1, tt * 128:(tt + 1) * 128],
                            BPLO[0:1, db * DW:(db + 1) * DW],
                            start=False, stop=True),
                            reads=["ones2b", "bplo"], writes=[ppk])
                        P.op("v", lambda e, tt=tt, db=db, psp=psp: e.tensor_tensor(
                            ACC[tt][:, db * DW:(db + 1) * DW],
                            ACC[tt][:, db * DW:(db + 1) * DW],
                            psp[:, 0:DW], ALU.add),
                            reads=[ppk, f"acc{tt}"], writes=[f"acc{tt}"])

        mid.close()
        with ExitStack() as ffn:
            fsb = lambda n, sh, t: ffn.enter_context(nc.sbuf_tensor(n, sh, t))
            fpt = lambda n, sh: ffn.enter_context(nc.psum_tensor(n, sh, F32))
            W2B1 = fsb("w2b", [128, KT * d], BF16)
            W1B1 = fsb("w1b", [128, KC * FF], BF16)
            XGT = [fsb(f"xgt{k}", [128, PAD], BF16) for k in range(KC)]
            H1G2 = [[fsb(f"h1g{b}_{k}", [128, PAD], BF16) for k in range(KT)]
                    for b in range(2)]
            XGR = [fsb(f"xgr{i}", [128, d], BF16) for i in range(2)]
            XH2S = [fsb(f"xh2s{i}", [128, d], BF16) for i in range(2)]
            FH = None
            FS = [psS[0], psS[1], psC, psD]
            FSK = ["psS0", "psS1", "psC", "psD"]
            ytall_f = [f"yt{k}" for k in range(KC)]
            accl = f"acc{TT - 1}"

            def stage_w1(e_next, first=False):
                P.op("g", lambda e, e_i=e_next: e.dma_start(
                    out=W1B1.ap().rearrange("p (k f) -> p k f", f=FF),
                    in_=bass.AP(w1, e_i * d * FF,
                                [[FF, 128], [128 * FF, KC], [1, FF]])),
                    reads=["w1b"] + ([accl] if first else []),
                    writes=["w1b"], kind="dma")

            def load_xgr(e_i, q4s, first=False):
                for q4 in q4s:
                    xr = XGR[q4 % 2]
                    xrk = f"xgr{q4 % 2}"
                    P.op("q", lambda e, e_i=e_i, q4=q4, xr=xr: e.dma_start(
                        out=xr[:],
                        in_=xgd.ap()[e_i * PAD + q4 * 128:e_i * PAD + (q4 + 1) * 128, :]),
                        reads=["xgd", xrk] + ([accl] if first and q4 == 0 else []),
                        writes=[xrk], kind="dma")

            stage_w1(0, first=True)
            load_xgr(0, range(min(2, NP4)), first=True)
            for e_i in range(E):
                P.op("g", lambda e, e_i=e_i: e.dma_start(
                    out=W2B1.ap().rearrange("p (k f) -> p k f", f=d),
                    in_=bass.AP(w2, e_i * FF * d,
                                [[d, 128], [128 * d, KT], [1, d]])),
                    reads=["w2b"] + ([accl] if e_i == 0 else []),
                    writes=["w2b"], kind="dma")
                for q4 in range(NP4):
                    if q4 >= 2:
                        load_xgr(e_i, [q4])
                    xr = XGR[q4 % 2]
                    xrk = f"xgr{q4 % 2}"
                    fsb16 = FS[q4 % 2].ap().bitcast(BF16)
                    fsk = FSK[q4 % 2]
                    for j in range(KC):
                        P.op("t", lambda e, j=j, xr=xr, fsb16=fsb16: e.transpose(
                            fsb16[:, j * 128:(j + 1) * 128],
                            xr[:, j * 128:(j + 1) * 128], IDB[:]),
                            reads=[xrk, "idb"], writes=[fsk])
                    for j in range(KC):
                        P.op("v", lambda e, j=j, q4=q4, fsb16=fsb16: e.tensor_copy(
                            out=XGT[j][:, q4 * 128:(q4 + 1) * 128],
                            in_=fsb16[:, j * 128:(j + 1) * 128]),
                            reads=[fsk], writes=[f"xgt{j}"])
                H1G = H1G2[e_i % 2]
                hb = e_i % 2
                for fm in range(KT):
                    j = fm % 2
                    for kc in range(KC):
                        P.op("t", lambda e, fm=fm, kc=kc, j=j:
                             e.matmul(FS[j][:, 0:PAD],
                                      W1B1[:, kc * FF + fm * 128:
                                           kc * FF + (fm + 1) * 128],
                                      XGT[kc][:],
                                      start=(kc == 0), stop=(kc == KC - 1)),
                             reads=["w1b", f"xgt{kc}"], writes=[FSK[j]])
                    P.op("s", lambda e, fm=fm, j=j, H1G=H1G: e.activation(
                        H1G[fm][:], FS[j][:, 0:PAD], AF.Gelu),
                        reads=[FSK[j], f"h1g{hb}_{fm}"], writes=[f"h1g{hb}_{fm}"])
                if e_i + 1 < E:
                    stage_w1(e_i + 1)
                    load_xgr(e_i + 1, range(min(2, NP4)))
                for lt in range(NP4):
                    xo = XH2S[lt % 2]
                    xok = f"xh2s{lt % 2}"
                    for kt in range(KT):
                        for db in range(DBLK):
                            j = 2 + db % 2
                            P.op("t", lambda e, kt=kt, lt=lt, db=db, j=j, H1G=H1G:
                                 e.matmul(FS[j][:, 0:DW],
                                          H1G[kt][:, lt * 128:(lt + 1) * 128],
                                          W2B1[:, kt * d + db * DW:
                                               kt * d + (db + 1) * DW],
                                          start=(kt == 0), stop=(kt == KT - 1)),
                                 reads=[f"h1g{hb}_{kt}", "w2b"], writes=[FSK[j]])
                    for db in range(DBLK):
                        j = 2 + db % 2
                        P.op("s", lambda e, db=db, j=j, xo=xo: e.copy(
                            xo[:, db * DW:(db + 1) * DW], FS[j][:, 0:DW]),
                            reads=[FSK[j], xok], writes=[xok])
                    P.op("g", lambda e, e_i=e_i, lt=lt, xo=xo: e.dma_start(
                        out=xh2d.ap()[e_i * PAD + lt * 128:e_i * PAD + (lt + 1) * 128, :],
                        in_=xo[:]),
                        reads=[xok], writes=["xh2d"], kind="dma")

            FH4 = [XGR[0], XGR[1], XH2S[0], XH2S[1]]
            FHK = ["xgr0", "xgr1", "xh2s0", "xh2s1"]
            for i in range(min(4, TT)):
                P.op("v", lambda e, fh=FH4[i]: e.memset(fh[:], 0.0),
                     reads=[FHK[i]], writes=[FHK[i]])
            for tt in range(TT):
                fh = FH4[tt % 4]
                fhk = FHK[tt % 4]
                P.op("g", lambda e, tt=tt, fh=fh: e.indirect_dma_start(
                    out=fh[:],
                    out_offset=None,
                    in_=xh2d.ap(),
                    in_offset=IndirectOffsetOnAxis(SLOTI[:, tt:tt + 1], 0),
                    bounds_check=E * PAD - 1,
                    oob_is_err=False),
                    reads=["xh2d", "sloti", fhk], writes=[fhk], kind="dma")
                P.op("v", lambda e, tt=tt, fh=fh: e.scalar_tensor_tensor(
                    ACC[tt][:], fh[:], WVS[:, tt:tt + 1], ACC[tt][:],
                    ALU.mult, ALU.add),
                    reads=[fhk, "wvs", f"acc{tt}"], writes=[f"acc{tt}"])
                if tt + 4 < TT:
                    P.op("v", lambda e, fh=fh: e.memset(fh[:], 0.0),
                         reads=[fhk], writes=[fhk])
                P.op("q", lambda e, tt=tt: e.dma_start(
                    out=y_out[tt * 128:(tt + 1) * 128, :], in_=ACC[tt][:]),
                    reads=[f"acc{tt}"], writes=[f"y{tt}"], kind="dma")

        from contextlib import ExitStack as _ES
        semstack = _ES()
        sems = {}
        for n in list(P.cnt):
            sems[n] = semstack.enter_context(nc.semaphore("S_" + n))
        with semstack, nc.Block() as block:
            g_finals = [(f"g{r}", P.cnt[f"g{r}"]) for r in range(P.DMA_RING)
                        if P.cnt[f"g{r}"] > 0]
            q_finals = [(f"q{r}", P.cnt[f"q{r}"]) for r in range(P.DMA_RING)
                        if P.cnt[f"q{r}"] > 0]

            @block.tensor
            def _(e):
                _replay(P.streams["t"], e, sems)

            @block.vector
            def _(e):
                _replay(P.streams["v"], e, sems)

            @block.scalar
            def _(e):
                _replay(P.streams["s"], e, sems)

            @block.sync
            def _(e):
                _replay(P.streams["q"], e, sems)
                for sn, val in q_finals:
                    e.wait_ge(sems[sn], val)

            @block.gpsimd
            def _(e):
                _replay(P.streams["g"], e, sems)
                for sn, val in g_finals:
                    e.wait_ge(sems[sn], val)

    return nc


def make_consts(cfg, slopes, b_kqv, b_proj):
    n_tok = cfg["n_tok"]; d = cfg["d"]; H = cfg["H"]; dh = cfg["dh"]
    E = cfg["E"]; KI = n_tok // 128
    f32 = np.float32
    c = {}
    c["c_idb"] = np.eye(128).astype(ml_dtypes.bfloat16)
    c["c_idf"] = np.eye(128, dtype=f32)
    cm = np.zeros((128, 128), f32)
    pi = np.arange(128)
    cm[pi[:, None] > pi[None, :]] = NEG
    c["c_cm"] = cm
    c["c_eio"] = np.ascontiguousarray(
        np.broadcast_to(np.arange(E, dtype=f32), (128, E)))
    c["c_eio99"] = np.ascontiguousarray(c["c_eio"] - 99.0).astype(f32)
    c["c_kidx"] = np.ascontiguousarray(
        (pi[:, None] + 128 * np.arange(KI)[None, :]).astype(f32))
    c["c_iota"] = np.ascontiguousarray(np.arange(n_tok, dtype=f32)[None, :])
    c["c_ones1f"] = np.ones((1, 128), f32)
    c["c_eps"] = np.full((128, 1), 1e-5, f32)
    c["c_slb"] = np.ascontiguousarray(
        np.broadcast_to(np.asarray(slopes, f32)[None, :], (128, H)))
    bkq = np.zeros((128, H), f32)
    bk = np.asarray(b_kqv, f32)
    for h in range(H):
        bkq[0:dh, h] = bk[h * dh:(h + 1) * dh]
        bkq[64:64 + dh, h] = bk[d + h * dh:d + (h + 1) * dh]
    c["c_bkq"] = bkq
    c["c_bv"] = np.ascontiguousarray(bk[2 * d:3 * d][None, :])
    c["c_bp"] = np.ascontiguousarray(np.asarray(b_proj, f32)[None, :])
    c["c_ones2b"] = np.ones((1, n_tok)).astype(ml_dtypes.bfloat16)
    trils = (pi[:, None] < pi[None, :]).astype(np.float32)
    c["c_trils"] = trils.astype(ml_dtypes.bfloat16)
    c["c_zeros"] = np.zeros((128, d), dtype=ml_dtypes.bfloat16)
    return c


def make_in_maps(cfg, x, wkqv, b_kqv, wproj, b_proj, slopes, wgm, w1m, w2m):
    NC = cfg["n_cores"]
    consts = make_consts(cfg, slopes, b_kqv, b_proj)
    shared = dict(
        wkqv=np.ascontiguousarray(np.asarray(wkqv, np.float32)),
        wproj=np.ascontiguousarray(np.asarray(wproj, np.float32)),
        wg=np.ascontiguousarray(np.asarray(wgm, np.float32)),
        w1=np.ascontiguousarray(np.asarray(w1m, np.float32)),
        w2=np.ascontiguousarray(np.asarray(w2m, np.float32)),
        **consts)
    x = np.asarray(x, np.float32)
    return [dict(shared, x=np.ascontiguousarray(x[c])) for c in range(NC)]


def kernel(L0_attn_x, L0_attn_W_kqv, L0_attn_b_kqv, L0_attn_W_proj,
           L0_attn_b_proj, L0_attn_slopes, L0_ffn_Wg, L0_ffn_W1, L0_ffn_W2):
    cfg = full_cfg()
    NC = cfg["n_cores"]
    in_maps = make_in_maps(cfg, L0_attn_x, L0_attn_W_kqv, L0_attn_b_kqv,
                           L0_attn_W_proj, L0_attn_b_proj, L0_attn_slopes,
                           L0_ffn_Wg, L0_ffn_W1, L0_ffn_W2)
    nc = build_program(cfg)
    res = run_bass_kernel_spmd(nc, in_maps, core_ids=list(range(NC)))
    out = np.stack([np.asarray(res.results[c]["y"], np.float32)
                    for c in range(NC)], axis=0)
    return out
